# version 21
# baseline (speedup 1.0000x reference)
import numpy as np
import ml_dtypes
from contextlib import ExitStack
import os as _os
import concourse.bass as bass
import concourse.mybir as mybir
from concourse.bass_utils import run_bass_kernel_spmd

SERIAL = set((_os.environ.get("MK_SERIAL") or "").split(","))

F32 = mybir.dt.float32
BF16 = mybir.dt.bfloat16
I32 = mybir.dt.int32
I8 = mybir.dt.int8
I16 = mybir.dt.int16
ALU = mybir.AluOpType
AFT = mybir.ActivationFunctionType
AXX = mybir.AxisListType.X

D = 2048
FF = 8192
ND = D // 128
NF = FF // 128
NSL = FF // 512
EPS = 1e-5
EXPMASK = 0x7F800000


class MK:
    def __init__(self, T=2048, debug=False):
        self.T = T
        self.debug = debug
        self.NT = T // 128
        nc = bass.Bass("TRN2", target_bir_lowering=False, detect_race_conditions=False)
        self.nc = nc
        self.es = ExitStack()
        dr = lambda n, sh, dt, k: nc.dram_tensor(n, sh, dt, kind=k).ap()
        self.x0 = dr("x0", [T, D], F32, "ExternalInput")
        self.watd = dr("wat", [3 * D, D], F32, "ExternalInput")
        self.wod = dr("wo", [D, D], F32, "ExternalInput")
        self.w1 = dr("w1", [FF, D], F32, "ExternalInput")
        self.w2 = dr("w2", [D, FF], F32, "ExternalInput")
        self.lnw128 = dr("lnw128", [128, D], F32, "ExternalInput")
        self.lnb128 = dr("lnb128", [128, D], F32, "ExternalInput")
        self.b1v = dr("b1v", [1, FF], F32, "ExternalInput")
        self.b2v = dr("b2v", [1, D], F32, "ExternalInput")
        self.ones2d = dr("ones2d", [2, 128], BF16, "ExternalInput")
        self.maskpcd = dr("maskpcd", [128, 256], F32, "ExternalInput")
        self.epscd = dr("epscd", [128, 1], F32, "ExternalInput")
        self.outd = dr("out", [T, D], F32, "ExternalOutput")
        dk = "ExternalOutput" if debug else "Internal"
        self.x1s = dr("x1s", [T, D], F32, dk)
        self.x2s = dr("x2s", [T, D], F32, dk)
        self.q1_s = dr("q1_s", [T, D], BF16, dk)
        self.q2_s = dr("q2_s", [T, FF], BF16, dk)
        self.w1q_s = dr("w1q_s", [FF, D], BF16, dk)
        self.w2q_s = dr("w2q_s", [D, FF], BF16, dk)
        self.at_s = dr("at_s", [T, D], BF16, dk)
        self.qkv_s = dr("qkv_s", [T, 3 * D], BF16, dk)
        self.watb_s = dr("watb_s", [3 * D, D], BF16, "Internal")
        self.wob_s = dr("wob_s", [D, D], BF16, "Internal")
        self.b1hl_s = dr("b1hl_s", [2, FF], BF16, "Internal")
        self.b2hl_s = dr("b2hl_s", [2, D], BF16, "Internal")
        self.S = {}
        self.n = {}
        sb = self.sb
        self.ones2 = sb("ones2", [2, 128], BF16)
        self.maskpc = sb("maskpc", [128, 256], F32)
        self.epsc = sb("epsc", [128, 1], F32)
        self.lnsave = sb("lnsave", [128, 2 * self.NT], F32)
        self.qam = [sb(f"qam_{i}", [128, 64], F32) for i in range(2)]
        self.qse = [sb(f"qse_{i}", [128, 64], F32) for i in range(2)]
        self.qiv = [sb(f"qiv_{i}", [128, 64], F32) for i in range(2)]
        self.qsc = [sb(f"qsc_{i}", [128, 64], F32) for i in range(2)]
        self.st24 = [sb(f"st24_{i}", [128, 24], F32) for i in range(2)]
        self.mv2 = [sb(f"mv2_{i}", [128, 2], F32) for i in range(2)]
        self.rt = [sb(f"rt_{i}", [128, 1], F32) for i in range(2)]
        self.rt2 = [sb(f"rt2_{i}", [128, 1], F32) for i in range(2)]
        self.q8 = sb("q8", [128, 2 * D], I8)
        ps = lambda n: self.es.enter_context(nc.psum_tensor(n, [128, 512], F32))
        self.psA = ps("psA"); self.psB = ps("psB")
        self.psC = ps("psC"); self.psD = ps("psD")
        self.psS1 = ps("psS1"); self.psS2 = ps("psS2")
        self.psV1 = ps("psV1"); self.psV2 = ps("psV2")
        self.blk = self.es.enter_context(nc.Block())

    def sem(self, name):
        if name not in self.S:
            self.S[name] = self.es.enter_context(self.nc.semaphore("sem_" + name))
            self.n[name] = 0
        return self.S[name]

    def sb(self, name, shape, dt, es=None):
        self._uc = getattr(self, '_uc', 0) + 1
        return (es or self.es).enter_context(
            self.nc.sbuf_tensor(f"{name}_u{self._uc}", shape, dt))

    def inc(self, instr, sem, k=1, eng=None):
        self.sem(sem)
        if eng is not None:
            instr = eng.drain()
        instr.then_inc(self.S[sem], k)
        self.n[sem] += k
        return self.n[sem]

    def dma(self, eng, sem, out, in_, transpose=False):
        self.sem(sem)
        if transpose:
            ins = eng.dma_start(out=out, in_=in_, transpose=True)
        else:
            ins = eng.dma_start(out=out, in_=in_)
        return self.inc(ins, sem, 16)

    def quiesce(self):
        snap = dict(self.n)
        for en in ("sync", "scalar", "vector", "tensor", "gpsimd"):
            def f(e, snap=snap):
                for s, v in snap.items():
                    if v:
                        e.wait_ge(self.S[s], v)
            getattr(self.blk, en)(f)

    def _qws(self, par, nb, bf):
        if bf:
            return (self.qam[par][:].bitcast(BF16)[:, 0:nb],
                    self.qse[par][:].bitcast(BF16)[:, 0:nb],
                    self.qiv[par][:].bitcast(BF16)[:, 0:nb],
                    self.qsc[par][:].bitcast(BF16)[:, 0:nb])
        return (self.qam[par][:, 0:nb], self.qse[par][:, 0:nb],
                self.qiv[par][:, 0:nb], self.qsc[par][:, 0:nb])

    def emit_quant8(self, v, src, dst, n, par, bf=False):
        nb = n // 32
        am, se, iv, sc = self._qws(par, nb, bf)
        idt, msk = (I16, 0x7F80) if bf else (I32, EXPMASK)
        q8 = self.q8[:, 0:n]
        bv = lambda ap: ap.rearrange("p (nb b) -> p nb b", b=32)
        bb = lambda ap: ap.unsqueeze(2).broadcast_to([128, nb, 32])
        with self.nc.allow_low_precision(reason="pow2 scales exact in bf16"):
            v.tensor_reduce(am, bv(src), AXX, ALU.max, apply_absolute_value=True)
            v.drain()
            v.tensor_scalar(se.bitcast(idt), am.bitcast(idt), msk, None,
                            ALU.bitwise_and)
            v.drain()
            v.reciprocal(iv, se)
            v.tensor_scalar(sc, se, 1.0 / 64.0, None, ALU.mult)
            v.drain()
            v.scalar_tensor_tensor(bv(q8), bb(iv), 64.0, bv(src), ALU.mult,
                                   ALU.mult)
            return v.tensor_tensor(bv(dst), bv(q8), bb(sc), ALU.mult)

    def emit_quant8_pair(self, v, src0, dst0, src1, dst1, n, bf=False):
        nb = n // 32
        idt, msk = (I16, 0x7F80) if bf else (I32, EXPMASK)
        bv = lambda ap: ap.rearrange("p (nb b) -> p nb b", b=32)
        bb = lambda ap: ap.unsqueeze(2).broadcast_to([128, nb, 32])
        ws = []
        for par in range(2):
            ws.append(self._qws(par, nb, bf) + (self.q8[:, n * par:n * (par + 1)],))
        srcs = (src0, src1)
        dsts = (dst0, dst1)
        with self.nc.allow_low_precision(reason="pow2 scales exact in bf16"):
            for par in range(2):
                v.tensor_reduce(ws[par][0], bv(srcs[par]), AXX, ALU.max,
                                apply_absolute_value=True)
            v.drain()
            for par in range(2):
                v.tensor_scalar(ws[par][1].bitcast(idt), ws[par][0].bitcast(idt),
                                msk, None, ALU.bitwise_and)
            v.drain()
            for par in range(2):
                v.reciprocal(ws[par][2], ws[par][1])
                v.tensor_scalar(ws[par][3], ws[par][1], 1.0 / 64.0, None, ALU.mult)
            v.drain()
            for par in range(2):
                v.scalar_tensor_tensor(bv(ws[par][4]), bb(ws[par][2]), 64.0,
                                       bv(srcs[par]), ALU.mult, ALU.mult)
            ins = None
            for par in range(2):
                ins = v.tensor_tensor(bv(dsts[par]), bv(ws[par][4]),
                                      bb(ws[par][3]), ALU.mult)
            return ins

    def emit_ln0(self, src_ap, par, t, tok_src_sem, tok_src):
        nt = {}

        def fv(v, t=t, par=par):
            v.wait_ge(self.S[tok_src_sem], tok_src)
            for c in range(4):
                v.bn_stats(self.st24[par][:, 6 * c:6 * c + 6],
                           src_ap[:, 512 * c:512 * (c + 1)])
            v.drain()
            ins = v.bn_aggr(self.lnsave[:, 2 * t:2 * t + 2], self.st24[par][:])
            nt['v'] = self.inc(ins, 'dv', eng=v)
        self.blk.vector(fv)

        def fs(s, t=t, par=par):
            s.wait_ge(self.S['dv'], nt['v'])
            ins = s.activation(self.rt[par][:], self.lnsave[:, 2 * t + 1:2 * t + 2],
                               AFT.Sqrt, bias=self.epsc[:, 0:1])
            nt['a'] = self.inc(ins, 'ac')
        self.blk.scalar(fs)
        return nt

    def emit_wcasts(self, i=None):
        chunks = list(range(16)) if i is None else [i]

        def fg(g, chunks=chunks):
            for c in chunks:
                if c < 12:
                    self.dma(g, 'wcst', self.watb_s[512 * c:512 * (c + 1), :],
                             self.watd[512 * c:512 * (c + 1), :])
                else:
                    cc = c - 12
                    self.dma(g, 'wcst', self.wob_s[512 * cc:512 * (cc + 1), :],
                             self.wod[512 * cc:512 * (cc + 1), :])
        self.blk.gpsimd(fg)

    def p0(self):
        nt = {}
        with ExitStack() as es:
            b1f = self.sb("b1f", [1, FF], F32, es)
            b2f = self.sb("b2f", [1, D], F32, es)
            b1h = self.sb("b1h", [1, FF], BF16, es)
            b2h = self.sb("b2h", [1, D], BF16, es)
            b1l = self.sb("b1l", [1, FF], BF16, es)
            b2l = self.sb("b2l", [1, D], BF16, es)

            def fsync(sq):
                self.dma(sq, 'cst', self.ones2[:], self.ones2d[:])
                self.dma(sq, 'cst', self.maskpc[:], self.maskpcd[:])
                self.dma(sq, 'cst', self.epsc[:], self.epscd[:])
                self.dma(sq, 'cst', b1f[:], self.b1v[:])
                nt['b'] = self.dma(sq, 'cst', b2f[:], self.b2v[:])
            self.blk.sync(fsync)

            def fv(v):
                v.wait_ge(self.S['cst'], nt['b'])
                v.tensor_copy(b1h[:], b1f[:])
                v.tensor_copy(b2h[:], b2f[:])
                v.drain()
                v.tensor_tensor(b1f[:], b1f[:], b1h[:], ALU.subtract)
                v.tensor_tensor(b2f[:], b2f[:], b2h[:], ALU.subtract)
                v.drain()
                v.tensor_copy(b1l[:], b1f[:])
                ins = v.tensor_copy(b2l[:], b2f[:])
                nt['v'] = self.inc(ins, 'dv', eng=v)
            self.blk.vector(fv)

            def fst(sp):
                sp.wait_ge(self.S['dv'], nt['v'])
                self.dma(sp, 'cst', self.b1hl_s[0:1, :], b1h[:])
                self.dma(sp, 'cst', self.b1hl_s[1:2, :], b1l[:])
                self.dma(sp, 'cst', self.b2hl_s[0:1, :], b2h[:])
                self.dma(sp, 'cst', self.b2hl_s[1:2, :], b2l[:])
            self.blk.scalar(fst)
        self.quiesce()

    def p_ffn_ab(self, first, x_in):
        NT, T = self.NT, self.T
        with ExitStack() as es:
            lnw = self.sb("lnw", [128, D], F32, es)
            lnb = self.sb("lnb", [128, D], F32, es)
            b1hl = self.sb("b1hl", [2, FF], BF16, es)
            self.q1T = self.sb("q1T", [128, ND * T], BF16, es)
            q1Tv = self.q1T[:].rearrange("p (c t) -> p c t", t=T)
            NW = 2 if first else 3
            wbuf = [self.sb(f"wbuf{i}", [128, ND * 512], BF16, es)
                    for i in range(NW)]
            wbufv = [w[:].rearrange("p (c o) -> p c o", o=512) for w in wbuf]
            xa = [self.sb(f"xa{i}", [128, D], F32, es) for i in range(2)]
            xbt = self.sb("xbt", [128, D], BF16, es)
            h0 = self.sb("h0", [128, D], BF16, es)
            q1b = [self.sb(f"q1b{i}", [128, D], BF16, es) for i in range(2)]
            wn = [self.sb(f"wn{i}", [128, D], F32, es) for i in range(2)] \
                if first else None
            wqv = [w[:].bitcast(BF16)[:, 0:D] for w in wn] if first else None
            gsb4 = self.sb("gsb4", [128, 2048], BF16, es)
            q2b4 = self.sb("q2b4", [128, 2048], BF16, es)

            def fsync(sq):
                self.dma(sq, 'cst', lnw[:], self.lnw128[:])
                self.dma(sq, 'cst', lnb[:], self.lnb128[:])
                self.dma(sq, 'cst', b1hl[:], self.b1hl_s[:])
            self.blk.sync(fsync)
            tok_cst = self.n['cst']

            guard_xa = [None, None]
            guard_q1b = [None, None]
            m1tok_sl0 = [None] * NT
            q1l_tok = [None] * NT

            def stageA(t):
                par = t % 2

                def fload(sq, t=t, par=par, g=guard_xa[par]):
                    if g is not None:
                        sq.wait_ge(self.S['dv'], g)
                    self.dma(sq, f'xa{par}', xa[par][:],
                             x_in[128 * t:128 * (t + 1), :])
                self.blk.sync(fload)
                tok_x = self.n[f'xa{par}']

                nt0 = self.emit_ln0(xa[par][:], par, t, f'xa{par}', tok_x)
                nt = {}

                def fvB(v, t=t, par=par, nt0=nt0):
                    v.wait_ge(self.S['ac'], nt0['a'])
                    v.reciprocal(self.lnsave[:, 2 * t + 1:2 * t + 2], self.rt[par][:])
                    v.drain()
                    v.tensor_scalar(xbt[:], xa[par][:],
                                    self.lnsave[:, 2 * t:2 * t + 1],
                                    self.lnsave[:, 2 * t + 1:2 * t + 2],
                                    ALU.subtract, ALU.mult)
                    for c in range(4):
                        v.bn_stats(self.st24[par][:, 6 * c:6 * c + 6],
                                   xbt[:, 512 * c:512 * (c + 1)])
                    v.drain()
                    ins = v.bn_aggr(self.mv2[par][:], self.st24[par][:])
                    nt['v'] = self.inc(ins, 'dv', eng=v)
                self.blk.vector(fvB)

                def fsB(s, par=par):
                    s.wait_ge(self.S['dv'], nt['v'])
                    ins = s.activation(self.rt2[par][:], self.mv2[par][:, 1:2],
                                       AFT.Sqrt, bias=self.epsc[:, 0:1])
                    nt['a'] = self.inc(ins, 'ac')
                self.blk.scalar(fsB)

                def fvC(v, t=t, par=par, gq=guard_q1b[par]):
                    v.wait_ge(self.S['ac'], nt['a'])
                    if gq is not None:
                        v.wait_ge(self.S[f'q1s{par}'], gq)
                    v.wait_ge(self.S['cst'], tok_cst)
                    v.reciprocal(self.rt2[par][:], self.rt2[par][:])
                    v.drain()
                    ut = xa[par][:]
                    v.tensor_scalar(ut, xbt[:], self.mv2[par][:, 0:1],
                                    self.rt2[par][:], ALU.subtract, ALU.mult)
                    v.tensor_tensor(ut, ut, lnw[:], ALU.mult)
                    v.tensor_tensor(h0[:], ut, lnb[:], ALU.add)
                    ins = self.emit_quant8(v, h0[:], q1b[par][:], D, par, bf=True)
                    nt['q'] = self.inc(ins, 'dv', eng=v)
                self.blk.vector(fvC)
                guard_xa[par] = nt['q']

                def fst(sp, t=t, par=par):
                    sp.wait_ge(self.S['dv'], nt['q'])
                    self.dma(sp, f'q1s{par}', self.q1_s[128 * t:128 * (t + 1), :],
                             q1b[par][:])
                self.blk.scalar(fst)
                guard_q1b[par] = self.n[f'q1s{par}']
                tok_q1st = self.n[f'q1s{par}']

                def fldT(sq, t=t, par=par, tok_q1st=tok_q1st,
                         g=(m1tok_sl0[t - 2] if t >= 2 else None)):
                    sq.wait_ge(self.S[f'q1s{par}'], tok_q1st)
                    if g is not None:
                        sq.wait_ge(self.S['pe'], g)
                    self.dma(sq, f'q1l{par}',
                             q1Tv[:, :, 128 * t:128 * (t + 1)],
                             self.q1_s[128 * t:128 * (t + 1), :].rearrange(
                                 "t (c p) -> t c p", p=128), transpose=True)
                    q1l_tok[t] = self.n[f'q1l{par}']
                self.blk.sync(fldT)

            w1_ready = [None] * NW
            mm_done = [None] * NW
            wn_guard = [None, None]
            wcnt = [0]

            def w_addr(kind, idx):
                if kind == 'w1':
                    sl, rt = idx
                    r0 = 512 * sl + 128 * rt
                    return self.w1[r0:r0 + 128, :], self.w1q_s[r0:r0 + 128, :]
                j, qc = idx // 4, idx % 4
                return (self.w2[128 * j:128 * (j + 1), 2048 * qc:2048 * (qc + 1)],
                        self.w2q_s[128 * j:128 * (j + 1), 2048 * qc:2048 * (qc + 1)])

            def w_pair(itA, itB):
                for par in range(2):
                    self.sem(f'w1s{par}')
                    self.sem(f'w2s{par}')
                items = (itA, itB)
                srcs = []
                dsts = []
                ssems = []
                for par, (kind, idx) in enumerate(items):
                    src, dst = w_addr(kind, idx)
                    srcs.append(src)
                    dsts.append(dst)
                    ssems.append(f'{kind}s{par}')

                    def fload(sq, par=par, src=src, g=wn_guard[par],
                              g1=self.n[f'w1s{par}'], g2=self.n[f'w2s{par}']):
                        if g is not None:
                            sq.wait_ge(self.S['dv'], g)
                        if g1:
                            sq.wait_ge(self.S[f'w1s{par}'], g1)
                        if g2:
                            sq.wait_ge(self.S[f'w2s{par}'], g2)
                        self.dma(sq, f'wn{par}', wn[par][:], src)
                    self.blk.sync(fload)
                tok0 = self.n['wn0']
                tok1 = self.n['wn1']
                nt = {}

                def fv(v, tok0=tok0, tok1=tok1):
                    v.wait_ge(self.S['wn0'], tok0)
                    v.wait_ge(self.S['wn1'], tok1)
                    ins = self.emit_quant8_pair(v, wn[0][:], wqv[0],
                                                wn[1][:], wqv[1], D)
                    nt['q'] = self.inc(ins, 'dv', eng=v)
                self.blk.vector(fv)
                wn_guard[0] = nt['q']
                wn_guard[1] = nt['q']

                def fst(sp):
                    sp.wait_ge(self.S['dv'], nt['q'])
                    self.dma(sp, ssems[0], dsts[0], wqv[0])
                    self.dma(sp, ssems[1], dsts[1], wqv[1])
                self.blk.scalar(fst)

            def w1_load(sl):
                bpar = sl % NW
                w0 = self.n.get('w1s0', 0)
                w1c = self.n.get('w1s1', 0)

                def fload(sq, sl=sl, bpar=bpar, mmd=mm_done[bpar], w0=w0, w1c=w1c):
                    if first:
                        if w0:
                            sq.wait_ge(self.S['w1s0'], w0)
                        if w1c:
                            sq.wait_ge(self.S['w1s1'], w1c)
                    if mmd is not None:
                        sq.wait_ge(self.S['pe'], mmd)
                    r0 = 512 * sl
                    self.dma(sq, f'wb{bpar}', wbufv[bpar],
                             self.w1q_s[r0:r0 + 512, :].rearrange(
                                 "o (c p) -> o c p", p=128), transpose=True)
                    w1_ready[bpar] = self.n[f'wb{bpar}']
                self.blk.sync(fload)

            psm1 = [self.psA, self.psB, self.psC, self.psD]
            gelu_done = [None] * 4
            qpair_guard = [None]
            q2st_done = [None, None]
            q2cnt = [0]

            def m1_tile(sl, t):
                par4 = t % 4
                bpar = sl % NW
                pmm = psm1[par4]

                def fmm(te, sl=sl, t=t, bpar=bpar, pmm=pmm, wtok=w1_ready[bpar],
                        g=gelu_done[par4], ql=q1l_tok[t], qpar=t % 2):
                    te.wait_ge(self.S['cst'], tok_cst)
                    te.wait_ge(self.S[f'wb{bpar}'], wtok)
                    te.wait_ge(self.S[f'q1l{qpar}'], ql)
                    if g is not None:
                        te.wait_ge(self.S['ac'], g)
                    for c in range(ND):
                        te.matmul(pmm[:], q1Tv[:, c, 128 * t:128 * (t + 1)],
                                  wbufv[bpar][:, c, :], start=(c == 0), stop=False)
                    ins = te.matmul(pmm[:], self.ones2[:],
                                    b1hl[:, 512 * sl:512 * (sl + 1)],
                                    start=False, stop=True)
                    self.inc(ins, 'pe')
                self.blk.tensor(fmm)
                tok_mm = self.n['pe']
                if t == NT - 1:
                    mm_done[bpar] = tok_mm
                if sl == 0:
                    m1tok_sl0[t] = tok_mm

                if t % 2 == 1:
                    nt = {}

                    def fg(s, t=t, tok_mm=tok_mm,
                           g=(qpair_guard[0] if (first and t % 4 == 1)
                              or not first else None)):
                        s.wait_ge(self.S['pe'], tok_mm)
                        if g is not None:
                            s.wait_ge(self.S['dv'], g)
                        s.activation(gsb4[:, 512 * ((t - 1) % 4):512 * ((t - 1) % 4) + 512],
                                     psm1[(t - 1) % 4][:], AFT.Gelu_apprx_tanh)
                        ins = s.activation(gsb4[:, 512 * (t % 4):512 * (t % 4) + 512],
                                           psm1[t % 4][:], AFT.Gelu_apprx_tanh)
                        nt['g'] = self.inc(ins, 'ac')
                    self.blk.scalar(fg)
                    gelu_done[(t - 1) % 4] = nt['g']
                    gelu_done[t % 4] = nt['g']

                if first and t % 4 == 3:
                    ntq = {}

                    def fq(v, g=nt['g'], g0=q2st_done[0]):
                        v.wait_ge(self.S['ac'], g)
                        if g0 is not None:
                            v.wait_ge(self.S['q2s0'], g0)
                        ins = self.emit_quant8(v, gsb4[:], q2b4[:], 2048, 0,
                                               bf=True)
                        ntq['q'] = self.inc(ins, 'dv', eng=v)
                    self.blk.vector(fq)
                    qpair_guard[0] = ntq['q']

                    def fst(sp, sl=sl, t=t, tok=ntq['q']):
                        sp.wait_ge(self.S['dv'], tok)
                        for k in range(4):
                            self.dma(sp, 'q2s0',
                                     self.q2_s[128 * (t - 3 + k):128 * (t - 2 + k),
                                               512 * sl:512 * (sl + 1)],
                                     q2b4[:, 512 * k:512 * (k + 1)])
                    self.blk.scalar(fst)
                    q2st_done[0] = self.n['q2s0']
                elif (not first) and t % 2 == 1:
                    ntq = {}
                    s0 = 512 * ((t - 1) % 4)
                    s1 = 512 * (t % 4)

                    def fq(v, g=nt['g'], g0=q2st_done[0], s0=s0, s1=s1):
                        v.wait_ge(self.S['ac'], g)
                        if g0 is not None:
                            v.wait_ge(self.S['q2s0'], g0)
                        ins = self.emit_quant8_pair(
                            v, gsb4[:, s0:s0 + 512], q2b4[:, 0:512],
                            gsb4[:, s1:s1 + 512], q2b4[:, 512:1024], 512, bf=True)
                        ntq['q'] = self.inc(ins, 'dv', eng=v)
                    self.blk.vector(fq)
                    qpair_guard[0] = ntq['q']

                    def fst(sp, sl=sl, t=t, tok=ntq['q']):
                        sp.wait_ge(self.S['dv'], tok)
                        self.dma(sp, 'q2s0',
                                 self.q2_s[128 * (t - 1):128 * t,
                                           512 * sl:512 * (sl + 1)],
                                 q2b4[:, 0:512])
                        self.dma(sp, 'q2s0',
                                 self.q2_s[128 * t:128 * (t + 1),
                                           512 * sl:512 * (sl + 1)],
                                 q2b4[:, 512:1024])
                    self.blk.scalar(fst)
                    q2st_done[0] = self.n['q2s0']

            witems = {sl: [] for sl in range(NSL)}
            if first:
                for sl in range(NSL - 1):
                    witems[sl].append((('w1', (sl + 1, 0)), ('w1', (sl + 1, 1))))
                    witems[sl].append((('w1', (sl + 1, 2)), ('w1', (sl + 1, 3))))
                for h in range(0, 32, 2):
                    witems[min(1 + h // 2, NSL - 1)].append((('w2', h), ('w2', h + 1)))

            def emit_wpair(pr):
                w_pair(pr[0], pr[1])
                if pr[1][0] == 'w1' and pr[1][1][1] == 3:
                    w1_load(pr[1][1][0])

            wl0 = witems[0]
            done0 = 0
            for t in range(NT):
                if 'pa' in SERIAL:
                    self.quiesce()
                stageA(t)
                if t == 0:
                    if first:
                        w_pair(('w1', (0, 0)), ('w1', (0, 1)))
                        w_pair(('w1', (0, 2)), ('w1', (0, 3)))
                    else:
                        w1_load(0)
                        w1_load(1)
                        w1_load(2)
                elif t == 1 and first:
                    w1_load(0)
                else:
                    want = (t * len(wl0) + NT - 2) // (NT - 1)
                    while done0 < min(want, len(wl0)):
                        emit_wpair(wl0[done0])
                        done0 += 1
                if t >= 1:
                    m1_tile(0, t - 1)
                if not first and t >= 2:
                    m1_tile(1, t - 2)
                if not first and t >= 3:
                    m1_tile(2, t - 3)
            m1_tile(0, NT - 1)
            while done0 < len(wl0):
                emit_wpair(wl0[done0])
                done0 += 1
            if not first:
                m1_tile(1, NT - 2)
                m1_tile(2, NT - 3)
                m1_tile(1, NT - 1)
                m1_tile(2, NT - 2)
                m1_tile(2, NT - 1)
                w1_load(3)

            for sl in range(3 if not first else 1, NSL):
                wl = witems[sl]
                done = 0
                if first and sl >= 1:
                    self.emit_wcasts(sl - 1)
                for t in range(NT):
                    if 'pb' in SERIAL:
                        self.quiesce()
                    if not first and t == 0 and sl + 1 < NSL:
                        w1_load(sl + 1)
                    want = ((t + 1) * len(wl) + NT - 1) // NT
                    while done < min(want, len(wl)):
                        emit_wpair(wl[done])
                        done += 1
                    m1_tile(sl, t)
                while done < len(wl):
                    emit_wpair(wl[done])
                    done += 1
            if first:
                self.emit_wcasts(15)
        self.quiesce()

    def p_ffn_c(self, first, x_in, x_out, out_sem):
        NT, T = self.NT, self.T
        with ExitStack() as es:
            ring = self.sb("ring", [128, NF * 1024], BF16, es)
            wnC = [self.sb(f"wnC{i}", [128, D], F32, es) for i in range(2)] \
                if first else None
            wqvC = [w[:].bitcast(BF16)[:, 0:D] for w in wnC] if first else None
            wn_guardC = [None, None]
            ringv = ring[:].rearrange("p (c d) -> p c d", d=1024)
            b2hl = self.sb("b2hl", [2, D], BF16, es)
            q2t = [self.sb(f"q2t{i}", [128, NF * 128], BF16, es) for i in range(2)]
            q2tv = [r[:].rearrange("p (c t) -> p c t", t=128) for r in q2t]
            xpc = [self.sb(f"xpc{i}", [128, 1024], F32, es) for i in range(2)]
            xbp1 = self.sb("xbp", [128, 1024], BF16, es)
            xbp = [xbp1, xbp1]
            h2p = self.sb("h2p", [128, 1024], BF16, es)
            xo = [self.sb(f"xo{i}", [128, 1024], F32, es) for i in range(2)]

            def fsync0(sq):
                self.dma(sq, 'cst', b2hl[:], self.b2hl_s[:])
            self.blk.sync(fsync0)
            tok_cst = self.n['cst']

            q2_tok = [None, None]
            q2_guard = [None, None]
            ps_guard = [None, None]
            xpc_guard = [None, None]
            xo_guard = [None, None]
            ring_guard = [None]
            q2s_tot = (self.n.get('q2s0', 0), self.n.get('q2s1', 0))
            w2s_tot = (self.n.get('w2s0', 0), self.n.get('w2s1', 0))

            def ring_load(dh):
                toks = {}
                wt = (self.n.get('w2s0', 0), self.n.get('w2s1', 0))

                def f(sq, dh=dh, g=ring_guard[0], wt=wt):
                    if first:
                        if wt[0]:
                            sq.wait_ge(self.S['w2s0'], wt[0])
                        if wt[1]:
                            sq.wait_ge(self.S['w2s1'], wt[1])
                    if g is not None:
                        sq.wait_ge(self.S['pe'], g)
                    src = self.w2q_s[1024 * dh:1024 * (dh + 1), :].rearrange(
                        "d (c p) -> d c p", p=128)
                    toks['a'] = self.dma(sq, 'rg', ringv[:, 0:32, :],
                                         src[:, 0:32, :], transpose=True)
                    toks['b'] = self.dma(sq, 'rg2', ringv[:, 32:64, :],
                                         src[:, 32:64, :], transpose=True)
                self.blk.sync(f)
                return (toks['a'], toks['b'])

            def w_pairC(hA, hB):
                for par in range(2):
                    self.sem(f'w2s{par}')
                hs = (hA, hB)

                def fload(sq, hs=hs, g0=wn_guardC[0], g1=wn_guardC[1],
                          s0=self.n['w2s0'], s1=self.n['w2s1']):
                    for par in range(2):
                        g = (g0, g1)[par]
                        if g is not None:
                            sq.wait_ge(self.S['dv'], g)
                        if (s0, s1)[par]:
                            sq.wait_ge(self.S[f'w2s{par}'], (s0, s1)[par])
                        j, qc = hs[par] // 4, hs[par] % 4
                        self.dma(sq, f'wnc{par}', wnC[par][:],
                                 self.w2[128 * j:128 * (j + 1),
                                         2048 * qc:2048 * (qc + 1)])
                self.blk.sync(fload)
                tok0 = self.n['wnc0']
                tok1 = self.n['wnc1']
                nt = {}

                def fv(v, tok0=tok0, tok1=tok1):
                    v.wait_ge(self.S['wnc0'], tok0)
                    v.wait_ge(self.S['wnc1'], tok1)
                    ins = self.emit_quant8_pair(v, wnC[0][:], wqvC[0],
                                                wnC[1][:], wqvC[1], D)
                    nt['q'] = self.inc(ins, 'dv', eng=v)
                self.blk.vector(fv)
                wn_guardC[0] = nt['q']
                wn_guardC[1] = nt['q']

                def fst(sp, hs=hs):
                    sp.wait_ge(self.S['dv'], nt['q'])
                    for par in range(2):
                        j, qc = hs[par] // 4, hs[par] % 4
                        self.dma(sp, f'w2s{par}',
                                 self.w2q_s[128 * j:128 * (j + 1),
                                            2048 * qc:2048 * (qc + 1)],
                                 wqvC[par][:])
                self.blk.scalar(fst)

            def q2t_load(t):
                par = t % 2

                def f(sq, t=t, par=par, g=q2_guard[par]):
                    sq.wait_ge(self.S['q2s0'], q2s_tot[0])
                    if q2s_tot[1]:
                        sq.wait_ge(self.S['q2s1'], q2s_tot[1])
                    if g is not None:
                        sq.wait_ge(self.S['pe'], g)
                    self.dma(sq, f'q2l{par}', q2tv[par],
                             self.q2_s[128 * t:128 * (t + 1), :].rearrange(
                                 "t (c p) -> t c p", p=128), transpose=True)
                    q2_tok[par] = self.n[f'q2l{par}']
                self.blk.sync(f)

            for dh in range(2):
                tok_rg = ring_load(dh)
                if dh == 0:
                    q2t_load(0)
                for t in range(NT):
                    if 'pc' in SERIAL:
                        self.quiesce()
                    par = t % 2
                    pa = (self.psA, self.psS1)[par]
                    pb = (self.psB, self.psS2)[par]
                    if t + 1 < NT:
                        q2t_load(t + 1)
                    elif dh == 0:
                        q2t_load(0)
                    if first and dh == 0:
                        w_pairC(32 + 2 * t, 33 + 2 * t)

                    def fmm1(te, t=t, par=par, pa=pa, pb=pb, tok_rg=tok_rg,
                             q2r=q2_tok[par], g=ps_guard[par]):
                        te.wait_ge(self.S['rg'], tok_rg[0])
                        te.wait_ge(self.S[f'q2l{par}'], q2r)
                        te.wait_ge(self.S['cst'], tok_cst)
                        if g is not None:
                            te.wait_ge(self.S['dv'], g)
                        for c in range(32):
                            te.matmul(pa[:], q2tv[par][:, c, :],
                                      ringv[:, c, 0:512], start=(c == 0), stop=False)
                            te.matmul(pb[:], q2tv[par][:, c, :],
                                      ringv[:, c, 512:1024], start=(c == 0),
                                      stop=False)
                    self.blk.tensor(fmm1)

                    def fmm(te, dh=dh, par=par, pa=pa, pb=pb, tok_rg=tok_rg):
                        te.wait_ge(self.S['rg2'], tok_rg[1])
                        for c in range(32, NF):
                            te.matmul(pa[:], q2tv[par][:, c, :],
                                      ringv[:, c, 0:512], start=False, stop=False)
                            te.matmul(pb[:], q2tv[par][:, c, :],
                                      ringv[:, c, 512:1024], start=False,
                                      stop=False)
                        o0 = 1024 * dh
                        te.matmul(pa[:], self.ones2[:],
                                  b2hl[:, o0:o0 + 512], start=False, stop=True)
                        ins = te.matmul(pb[:], self.ones2[:],
                                        b2hl[:, o0 + 512:o0 + 1024],
                                        start=False, stop=True)
                        self.inc(ins, 'pe')
                    self.blk.tensor(fmm)
                    tok_mm = self.n['pe']
                    q2_guard[par] = tok_mm
                    if t == NT - 1:
                        ring_guard[0] = tok_mm

                    def fsx(sq, dh=dh, t=t, par=par, g=xpc_guard[par]):
                        if g is not None:
                            sq.wait_ge(self.S['dv'], g)
                        self.dma(sq, f'xp{par}', xpc[par][:],
                                 x_in[128 * t:128 * (t + 1),
                                      1024 * dh:1024 * (dh + 1)])
                    self.blk.sync(fsx)
                    tok_xp = self.n[f'xp{par}']
                    nt = {}

                    def fv(v, dh=dh, t=t, par=par, pa=pa, pb=pb, tok_mm=tok_mm,
                           tok_xp=tok_xp, gxo=xo_guard[par]):
                        v.wait_ge(self.S['pe'], tok_mm)
                        v.wait_ge(self.S[f'xp{par}'], tok_xp)
                        if gxo is not None:
                            v.wait_ge(self.S[f'{out_sem}{par}'], gxo)
                        v.tensor_scalar(xbp[par][:], xpc[par][:],
                                        self.lnsave[:, 2 * t:2 * t + 1],
                                        self.lnsave[:, 2 * t + 1:2 * t + 2],
                                        ALU.subtract, ALU.mult)
                        v.tensor_copy(h2p[:, 0:512], pa[:])
                        v.tensor_copy(h2p[:, 512:1024], pb[:])
                        v.tensor_tensor(h2p[:], xbp[par][:], h2p[:], ALU.add)
                        ins = v.scalar_tensor_tensor(xo[par][:], h2p[:], 0.5,
                                                     xpc[par][:], ALU.mult, ALU.add)
                        nt['u'] = self.inc(ins, 'dv', eng=v)
                    self.blk.vector(fv)
                    ps_guard[par] = nt['u']
                    xpc_guard[par] = nt['u']

                    def fso(sp, dh=dh, t=t, par=par, tok=nt['u']):
                        sp.wait_ge(self.S['dv'], tok)
                        self.dma(sp, f'{out_sem}{par}',
                                 x_out[128 * t:128 * (t + 1),
                                       1024 * dh:1024 * (dh + 1)],
                                 xo[par][:])
                        xo_guard[par] = self.n[f'{out_sem}{par}']
                    self.blk.scalar(fso)
        self.quiesce()

    def p_attn_qkv(self, x_in):
        NT, T = self.NT, self.T
        with ExitStack() as es:
            self.aT = self.sb("aT", [128, ND * T], BF16, es)
            aTv = self.aT[:].rearrange("p (c t) -> p c t", t=T)
            xa = [self.sb(f"xa{i}", [128, D], F32, es) for i in range(2)]
            at = [self.sb(f"at{i}", [128, D], BF16, es) for i in range(2)]
            wsl = [self.sb(f"wsl{i}", [128, ND * 512], BF16, es) for i in range(2)]
            wslv = [w[:].rearrange("p (c o) -> p c o", o=512) for w in wsl]
            qkvb = [self.sb(f"qkvb{i}", [128, 512], BF16, es) for i in range(2)]

            guard_xa = [None, None]
            guard_at = [None, None]
            aTl_tok = [None] * NT
            e0tok = [None] * NT

            def stageD(t):
                par = t % 2

                def fload(sq, t=t, par=par, g=guard_xa[par]):
                    if g is not None:
                        sq.wait_ge(self.S['dv'], g)
                    self.dma(sq, f'xa{par}', xa[par][:],
                             x_in[128 * t:128 * (t + 1), :])
                self.blk.sync(fload)
                tok_x = self.n[f'xa{par}']
                nt0 = self.emit_ln0(xa[par][:], par, t, f'xa{par}', tok_x)
                nt = {}

                def fvB(v, t=t, par=par, nt0=nt0, gq=guard_at[par]):
                    v.wait_ge(self.S['ac'], nt0['a'])
                    if gq is not None:
                        v.wait_ge(self.S[f'ats{par}'], gq)
                    v.reciprocal(self.lnsave[:, 2 * t + 1:2 * t + 2], self.rt[par][:])
                    v.drain()
                    ins = v.tensor_scalar(at[par][:], xa[par][:],
                                          self.lnsave[:, 2 * t:2 * t + 1],
                                          self.lnsave[:, 2 * t + 1:2 * t + 2],
                                          ALU.subtract, ALU.mult)
                    nt['v'] = self.inc(ins, 'dv', eng=v)
                self.blk.vector(fvB)
                guard_xa[par] = nt['v']

                def fst(sp, t=t, par=par):
                    sp.wait_ge(self.S['dv'], nt['v'])
                    self.dma(sp, f'ats{par}', self.at_s[128 * t:128 * (t + 1), :],
                             at[par][:])
                self.blk.scalar(fst)
                guard_at[par] = self.n[f'ats{par}']
                tok_st = self.n[f'ats{par}']

                def fldT(sq, t=t, par=par, tok_st=tok_st,
                         g=(e0tok[t - 2] if t >= 2 else None)):
                    sq.wait_ge(self.S[f'ats{par}'], tok_st)
                    if g is not None:
                        sq.wait_ge(self.S['pe'], g)
                    self.dma(sq, f'atl{par}',
                             aTv[:, :, 128 * t:128 * (t + 1)],
                             self.at_s[128 * t:128 * (t + 1), :].rearrange(
                                 "t (c p) -> t c p", p=128), transpose=True)
                    aTl_tok[t] = self.n[f'atl{par}']
                self.blk.sync(fldT)

            w_ready = [None, None]
            mm_done = [None, None]
            store_done = [None, None]
            wcst_tok = self.n['wcst']

            def wsl_load(os):
                par = os % 2

                def f(sq, os=os, par=par, g=mm_done[par]):
                    sq.wait_ge(self.S['wcst'], wcst_tok)
                    if g is not None:
                        sq.wait_ge(self.S['pe'], g)
                    self.dma(sq, f'ws{par}', wslv[par],
                             self.watb_s[512 * os:512 * (os + 1), :].rearrange(
                                 "o (c p) -> o c p", p=128), transpose=True)
                    w_ready[par] = self.n[f'ws{par}']
                self.blk.sync(f)

            psq = [self.psA, self.psB, self.psC, self.psD]
            ps_guard = [None] * 4

            def e_tile(os, t):
                par = os % 2
                par4 = t % 4
                pmm = psq[par4]

                def fmm(te, os=os, t=t, par=par, pmm=pmm, wt=w_ready[par],
                        g=ps_guard[par4], al=aTl_tok[t], apar=t % 2):
                    te.wait_ge(self.S[f'ws{par}'], wt)
                    te.wait_ge(self.S[f'atl{apar}'], al)
                    if g is not None:
                        te.wait_ge(self.S['dv'], g)
                    for c in range(ND):
                        ins = te.matmul(pmm[:], aTv[:, c, 128 * t:128 * (t + 1)],
                                        wslv[par][:, c, :],
                                        start=(c == 0), stop=(c == ND - 1))
                    self.inc(ins, 'pe')
                self.blk.tensor(fmm)
                tok_mm = self.n['pe']
                if t == self.NT - 1:
                    mm_done[par] = tok_mm
                if os == 0:
                    e0tok[t] = tok_mm
                epar = t % 2
                nt = {}

                def fv(v, os=os, pmm=pmm, epar=epar, tok_mm=tok_mm,
                       g=store_done[epar]):
                    v.wait_ge(self.S['pe'], tok_mm)
                    if g is not None:
                        v.wait_ge(self.S[f'qks{epar}'], g)
                    if os < 4:
                        ins = v.tensor_scalar(qkvb[epar][:], pmm[:], 1.0 / 128.0,
                                              None, ALU.mult)
                    else:
                        ins = v.tensor_copy(qkvb[epar][:], pmm[:])
                    nt['e'] = self.inc(ins, 'dv', eng=v)
                self.blk.vector(fv)
                ps_guard[par4] = nt['e']

                def fst(sp, os=os, t=t, epar=epar, tok=nt['e']):
                    sp.wait_ge(self.S['dv'], tok)
                    self.dma(sp, f'qks{epar}',
                             self.qkv_s[128 * t:128 * (t + 1),
                                        512 * os:512 * (os + 1)],
                             qkvb[epar][:])
                self.blk.scalar(fst)
                store_done[epar] = self.n[f'qks{epar}']

            for t in range(NT):
                if 'pd' in SERIAL:
                    self.quiesce()
                stageD(t)
                if t == 0:
                    wsl_load(0)
                if t == 1:
                    wsl_load(1)
                if t >= 1:
                    e_tile(0, t - 1)
                if t >= 2:
                    e_tile(1, t - 2)
            e_tile(0, NT - 1)
            e_tile(1, NT - 2)
            e_tile(1, NT - 1)
            wsl_load(2)
            for os in range(2, 12):
                for t in range(NT):
                    if 'pq' in SERIAL:
                        self.quiesce()
                    if t == 0 and os + 1 < 12:
                        wsl_load(os + 1)
                    e_tile(os, t)
        self.quiesce()

    def p_attn_win(self, x_in, x_out, out_sem):
        NT, T = self.NT, self.T
        with ExitStack() as es:
            oT = self.sb("oT", [128, ND * T], BF16, es)
            oTv = oT[:].rearrange("p (c t) -> p c t", t=T)
            kw = [self.sb(f"kw{i}", [128, ND * 128], BF16, es) for i in range(2)]
            qw = [self.sb(f"qw{i}", [128, ND * 128], BF16, es) for i in range(2)]
            vw = [self.sb(f"vw{i}", [128, D], BF16, es) for i in range(2)]
            kwv = [k[:].rearrange("p (c j) -> p c j", j=128) for k in kw]
            qwv = [k[:].rearrange("p (c j) -> p c j", j=128) for k in qw]
            rpc = self.sb("rpc", [128, 256], F32, es)
            at2 = self.sb("at2", [128, 256], BF16, es)

            kv_tok = [None, None]
            last_av = [None]
            relu_guard = [None]
            mask_guard = [None]
            evac_guard = [None, None]
            qks_tot = (self.n.get('qks0', 0), self.n.get('qks1', 0))

            def fz(v):
                v.memset(rpc[:], 0.0)
                ins = v.drain()
                self.inc(ins, 'dv')
            self.blk.vector(fz)

            def loadw(w):
                par = w % 2

                def f(sq, w=w, par=par, g=last_av[0]):
                    sq.wait_ge(self.S['qks0'], qks_tot[0])
                    sq.wait_ge(self.S['qks1'], qks_tot[1])
                    if g is not None:
                        sq.wait_ge(self.S['pe'], g)
                    row = self.qkv_s[128 * w:128 * (w + 1), :]
                    self.dma(sq, f'kv{par}', qwv[par],
                             row[:, 0:D].rearrange("t (c p) -> t c p", p=128),
                             transpose=True)
                    self.dma(sq, f'kv{par}', kwv[par],
                             row[:, D:2 * D].rearrange("t (c p) -> t c p", p=128),
                             transpose=True)
                    self.dma(sq, f'kv{par}', vw[par][:], row[:, 2 * D:3 * D])
                    kv_tok[par] = self.n[f'kv{par}']
                self.blk.sync(f)

            loadw(0)
            for w in range(NT):
                if 'pw' in SERIAL:
                    self.quiesce()
                par = w % 2
                havep = w > 0

                def fsim(te, w=w, par=par, havep=havep, rg=relu_guard[0],
                         kt=kv_tok[par]):
                    te.wait_ge(self.S[f'kv{par}'], kt)
                    if rg is not None:
                        te.wait_ge(self.S['ac'], rg)
                    if havep:
                        for c in range(ND):
                            te.matmul(self.psS1[:, 0:128], kwv[1 - par][:, c, :],
                                      qwv[par][:, c, :], start=(c == 0),
                                      stop=(c == ND - 1))
                    for c in range(ND):
                        ins = te.matmul(self.psS2[:, 0:128], kwv[par][:, c, :],
                                        qwv[par][:, c, :], start=(c == 0),
                                        stop=(c == ND - 1))
                    self.inc(ins, 'pe')
                self.blk.tensor(fsim)
                tok_sim = self.n['pe']

                def frelu(s, havep=havep, tok_sim=tok_sim, mg=mask_guard[0]):
                    s.wait_ge(self.S['pe'], tok_sim)
                    if mg is not None:
                        s.wait_ge(self.S['dv'], mg)
                    if havep:
                        s.activation(rpc[:, 0:128], self.psS1[:, 0:128], AFT.Relu)
                    ins = s.activation(rpc[:, 128:256], self.psS2[:, 0:128],
                                       AFT.Relu)
                    self.inc(ins, 'ac')
                self.blk.scalar(frelu)
                relu_guard[0] = self.n['ac']
                tok_relu = self.n['ac']
                nt = {}

                def fmask(v, tok_relu=tok_relu, avg=last_av[0]):
                    v.wait_ge(self.S['ac'], tok_relu)
                    if avg is not None:
                        v.wait_ge(self.S['pe'], avg)
                    v.tensor_tensor(rpc[:], rpc[:], self.maskpc[:], ALU.mult)
                    v.drain()
                    ins = v.tensor_tensor(at2[:], rpc[:], rpc[:], ALU.mult)
                    nt['m'] = self.inc(ins, 'dv', eng=v)
                self.blk.vector(fmask)
                mask_guard[0] = nt['m']
                tok_at = nt['m']

                for grp in range(4):
                    psv = (self.psV1, self.psV2)[grp % 2]
                    gpar = grp % 2

                    def fav(te, par=par, havep=havep, grp=grp, psv=psv,
                            tok_at=tok_at, ev=evac_guard[grp % 2]):
                        te.wait_ge(self.S['dv'], tok_at)
                        if ev is not None:
                            te.wait_ge(self.S['dv'], ev)
                        for j in range(4):
                            c = 4 * grp + j
                            if havep:
                                te.matmul(psv[:, 128 * j:128 * (j + 1)],
                                          vw[1 - par][:, 128 * c:128 * (c + 1)],
                                          at2[:, 0:128], start=True, stop=False)
                            ins = te.matmul(psv[:, 128 * j:128 * (j + 1)],
                                            vw[par][:, 128 * c:128 * (c + 1)],
                                            at2[:, 128:256], start=(not havep),
                                            stop=True)
                        self.inc(ins, 'pe')
                    self.blk.tensor(fav)
                    tok_av = self.n['pe']
                    if grp == 3:
                        last_av[0] = tok_av

                    def fve(v, w=w, grp=grp, psv=psv, tok_av=tok_av):
                        v.wait_ge(self.S['pe'], tok_av)
                        ins = v.tensor_copy(
                            oTv[:, 4 * grp:4 * grp + 4, 128 * w:128 * (w + 1)],
                            psv[:].rearrange("p (j tt) -> p j tt", tt=128))
                        self.inc(ins, 'dv', eng=v)
                    self.blk.vector(fve)
                    evac_guard[gpar] = self.n['dv']
                if w + 1 < NT:
                    loadw(w + 1)
            tok_oT = self.n['dv']

            wob = [self.sb(f"wob{i}", [128, ND * 512], BF16, es) for i in range(2)]
            wobv = [w[:].rearrange("p (c o) -> p c o", o=512) for w in wob]
            xpc = [self.sb(f"xpg{i}", [128, 512], F32, es) for i in range(2)]
            xo = [self.sb(f"xog{i}", [128, 512], F32, es) for i in range(2)]
            wo_ready = [None, None]
            wo_guard = [None, None]
            xo_guard = [None, None]
            xpc_guard = [None, None]
            psg = [None, None]
            wcst_tok = self.n['wcst']

            def wo_load(os):
                par = os % 2

                def f(sq, os=os, par=par, g=wo_guard[par]):
                    sq.wait_ge(self.S['wcst'], wcst_tok)
                    if g is not None:
                        sq.wait_ge(self.S['pe'], g)
                    self.dma(sq, f'wo{par}', wobv[par],
                             self.wob_s[512 * os:512 * (os + 1), :].rearrange(
                                 "o (c p) -> o c p", p=128), transpose=True)
                    wo_ready[par] = self.n[f'wo{par}']
                self.blk.sync(f)

            wo_load(0)
            for os in range(4):
                par = os % 2
                if os + 1 < 4:
                    wo_load(os + 1)
                for t in range(NT):
                    if 'pg' in SERIAL:
                        self.quiesce()
                    pmm = (self.psA, self.psB)[t % 2]
                    ppar = t % 2

                    def fsx(sq, os=os, t=t, ppar=ppar, g=xpc_guard[ppar]):
                        if g is not None:
                            sq.wait_ge(self.S['dv'], g)
                        self.dma(sq, f'xp{ppar}', xpc[ppar][:],
                                 x_in[128 * t:128 * (t + 1),
                                      512 * os:512 * (os + 1)])
                    self.blk.sync(fsx)
                    tok_xp = self.n[f'xp{ppar}']

                    def fmm(te, os=os, t=t, pmm=pmm, par=par, wt=wo_ready[par],
                            g=psg[ppar]):
                        te.wait_ge(self.S[f'wo{par}'], wt)
                        if os == 0 and t == 0:
                            te.wait_ge(self.S['dv'], tok_oT)
                        if g is not None:
                            te.wait_ge(self.S['dv'], g)
                        for c in range(ND):
                            ins = te.matmul(pmm[:], oTv[:, c, 128 * t:128 * (t + 1)],
                                            wobv[par][:, c, :],
                                            start=(c == 0), stop=(c == ND - 1))
                        self.inc(ins, 'pe')
                    self.blk.tensor(fmm)
                    tok_mm = self.n['pe']
                    if t == NT - 1:
                        wo_guard[par] = tok_mm
                    nt = {}

                    def fvu(v, pmm=pmm, ppar=ppar, tok_mm=tok_mm, tok_xp=tok_xp,
                            g=xo_guard[ppar]):
                        v.wait_ge(self.S['pe'], tok_mm)
                        v.wait_ge(self.S[f'xp{ppar}'], tok_xp)
                        if g is not None:
                            v.wait_ge(self.S[f'{out_sem}{ppar}'], g)
                        ins = v.tensor_tensor(xo[ppar][:], pmm[:], xpc[ppar][:],
                                              ALU.add)
                        nt['u'] = self.inc(ins, 'dv', eng=v)
                    self.blk.vector(fvu)
                    psg[ppar] = nt['u']
                    xpc_guard[ppar] = nt['u']

                    def fso(sp, os=os, t=t, ppar=ppar, tok=nt['u']):
                        sp.wait_ge(self.S['dv'], tok)
                        self.dma(sp, f'{out_sem}{ppar}',
                                 x_out[128 * t:128 * (t + 1),
                                       512 * os:512 * (os + 1)],
                                 xo[ppar][:])
                        xo_guard[ppar] = self.n[f'{out_sem}{ppar}']
                    self.blk.scalar(fso)
        self.quiesce()

    def build(self):
        self.p0()
        self.p_ffn_ab(True, self.x0)
        self.p_ffn_c(True, self.x0, self.x1s, 'p')
        self.p_attn_qkv(self.x1s)
        self.p_attn_win(self.x1s, self.x2s, 'o')
        self.p_ffn_ab(False, self.x2s)
        self.p_ffn_c(False, self.x2s, self.outd, 'f')
        snap = dict(self.n)

        def fin(sq, snap=snap):
            for s, v in snap.items():
                if v:
                    sq.wait_ge(self.S[s], v)
        self.blk.sync(fin)

    def finish(self):
        self.es.close()
        return self.nc


def build_program(T=2048, debug=False):
    mk = MK(T, debug=debug)
    mk.build()
    return mk.finish()


def host_prep(x_full, w_attn, w_o, ln_w, ln_b, W1, b1, W2, b2):
    f = np.float32
    bf = ml_dtypes.bfloat16
    common = {
        "wat": np.ascontiguousarray(w_attn.astype(f, copy=False)),
        "wo": np.ascontiguousarray(w_o.astype(f, copy=False)),
        "w1": np.ascontiguousarray(W1.astype(f, copy=False)),
        "w2": np.ascontiguousarray(W2.astype(f, copy=False)),
        "lnw128": np.ascontiguousarray(np.broadcast_to(ln_w[None, :], (128, D)).astype(f)),
        "lnb128": np.ascontiguousarray(np.broadcast_to(ln_b[None, :], (128, D)).astype(f)),
        "b1v": np.ascontiguousarray(b1[None, :].astype(f)),
        "b2v": np.ascontiguousarray(b2[None, :].astype(f)),
        "ones2d": np.ones((2, 128), dtype=bf),
        "maskpcd": np.ascontiguousarray(np.hstack([
            np.tril(np.ones((128, 128), dtype=f)),
            np.triu(np.ones((128, 128), dtype=f))])),
        "epscd": np.full((128, 1), EPS, dtype=f),
    }
    xs = [np.ascontiguousarray(x_full[i].astype(f, copy=False))
          for i in range(x_full.shape[0])]
    return common, xs


def kernel(L0_attn_x, L0_attn_w_attn, L0_attn_w_o, L0_ffn_ln_w, L0_ffn_ln_b,
           L0_ffn_W1, L0_ffn_b1, L0_ffn_W2, L0_ffn_b2):
    x = np.asarray(L0_attn_x)
    B, T, _ = x.shape
    common, xs = host_prep(x, np.asarray(L0_attn_w_attn), np.asarray(L0_attn_w_o),
                           np.asarray(L0_ffn_ln_w), np.asarray(L0_ffn_ln_b),
                           np.asarray(L0_ffn_W1), np.asarray(L0_ffn_b1),
                           np.asarray(L0_ffn_W2), np.asarray(L0_ffn_b2))
    nc = build_program(T)
    in_maps = [dict(common, x0=xs[i]) for i in range(B)]
    res = run_bass_kernel_spmd(nc, in_maps, core_ids=list(range(B)))
    return np.stack([np.asarray(r["out"]) for r in res.results], axis=0)


# version 23
# speedup vs baseline: 1.0601x; 1.0601x over previous
import numpy as np
import ml_dtypes
from contextlib import ExitStack
import os as _os
import concourse.bass as bass
import concourse.mybir as mybir
import concourse.bass_utils as _bu
from concourse.bass_utils import run_bass_kernel_spmd

if not getattr(_bu, "_ldw_opt_patched", False):
    _orig_run_command = _bu.run_command

    def _run_command_ldw(argv, **kwargs):
        argv = [a.replace("--enable-ldw-opt=false", "--enable-ldw-opt=true")
                if isinstance(a, str) else a for a in argv]
        return _orig_run_command(argv, **kwargs)

    _bu.run_command = _run_command_ldw
    _bu._ldw_opt_patched = True

SERIAL = set((_os.environ.get("MK_SERIAL") or "").split(","))

F32 = mybir.dt.float32
BF16 = mybir.dt.bfloat16
I32 = mybir.dt.int32
I8 = mybir.dt.int8
I16 = mybir.dt.int16
ALU = mybir.AluOpType
AFT = mybir.ActivationFunctionType
AXX = mybir.AxisListType.X

D = 2048
FF = 8192
ND = D // 128
NF = FF // 128
NSL = FF // 512
EPS = 1e-5
EXPMASK = 0x7F800000


class MK:
    def __init__(self, T=2048, debug=False):
        self.T = T
        self.debug = debug
        self.NT = T // 128
        nc = bass.Bass("TRN2", target_bir_lowering=False, detect_race_conditions=False)
        self.nc = nc
        self.es = ExitStack()
        dr = lambda n, sh, dt, k: nc.dram_tensor(n, sh, dt, kind=k).ap()
        self.x0 = dr("x0", [T, D], F32, "ExternalInput")
        self.watd = dr("wat", [3 * D, D], F32, "ExternalInput")
        self.wod = dr("wo", [D, D], F32, "ExternalInput")
        self.w1 = dr("w1", [FF, D], F32, "ExternalInput")
        self.w2 = dr("w2", [D, FF], F32, "ExternalInput")
        self.lnw128 = dr("lnw128", [128, D], F32, "ExternalInput")
        self.lnb128 = dr("lnb128", [128, D], F32, "ExternalInput")
        self.b1v = dr("b1v", [1, FF], F32, "ExternalInput")
        self.b2v = dr("b2v", [1, D], F32, "ExternalInput")
        self.ones2d = dr("ones2d", [2, 128], BF16, "ExternalInput")
        self.maskpcd = dr("maskpcd", [128, 256], F32, "ExternalInput")
        self.epscd = dr("epscd", [128, 1], F32, "ExternalInput")
        self.outd = dr("out", [T, D], F32, "ExternalOutput")
        dk = "ExternalOutput" if debug else "Internal"
        self.x1s = dr("x1s", [T, D], F32, dk)
        self.x2s = dr("x2s", [T, D], F32, dk)
        self.q1_s = dr("q1_s", [T, D], BF16, dk)
        self.q2_s = dr("q2_s", [T, FF], BF16, dk)
        self.w1q_s = dr("w1q_s", [FF, D], BF16, dk)
        self.w2q_s = dr("w2q_s", [D, FF], BF16, dk)
        self.at_s = dr("at_s", [T, D], BF16, dk)
        self.qkv_s = dr("qkv_s", [T, 3 * D], BF16, dk)
        self.watb_s = dr("watb_s", [3 * D, D], BF16, "Internal")
        self.wob_s = dr("wob_s", [D, D], BF16, "Internal")
        self.b1hl_s = dr("b1hl_s", [2, FF], BF16, "Internal")
        self.b2hl_s = dr("b2hl_s", [2, D], BF16, "Internal")
        self.S = {}
        self.n = {}
        sb = self.sb
        self.ones2 = sb("ones2", [2, 128], BF16)
        self.maskpc = sb("maskpc", [128, 256], F32)
        self.epsc = sb("epsc", [128, 1], F32)
        self.lnsave = sb("lnsave", [128, 2 * self.NT], F32)
        self.qam = [sb(f"qam_{i}", [128, 64], F32) for i in range(2)]
        self.qse = [sb(f"qse_{i}", [128, 64], F32) for i in range(2)]
        self.qiv = [sb(f"qiv_{i}", [128, 64], F32) for i in range(2)]
        self.qsc = [sb(f"qsc_{i}", [128, 64], F32) for i in range(2)]
        self.st24 = [sb(f"st24_{i}", [128, 24], F32) for i in range(2)]
        self.mv2 = [sb(f"mv2_{i}", [128, 2], F32) for i in range(2)]
        self.rt = [sb(f"rt_{i}", [128, 1], F32) for i in range(2)]
        self.rt2 = [sb(f"rt2_{i}", [128, 1], F32) for i in range(2)]
        self.q8 = sb("q8", [128, 2 * D], I8)
        ps = lambda n: self.es.enter_context(nc.psum_tensor(n, [128, 512], F32))
        self.psA = ps("psA"); self.psB = ps("psB")
        self.psC = ps("psC"); self.psD = ps("psD")
        self.psS1 = ps("psS1"); self.psS2 = ps("psS2")
        self.psV1 = ps("psV1"); self.psV2 = ps("psV2")
        self.blk = self.es.enter_context(nc.Block())

    def sem(self, name):
        if name not in self.S:
            self.S[name] = self.es.enter_context(self.nc.semaphore("sem_" + name))
            self.n[name] = 0
        return self.S[name]

    def sb(self, name, shape, dt, es=None):
        self._uc = getattr(self, '_uc', 0) + 1
        return (es or self.es).enter_context(
            self.nc.sbuf_tensor(f"{name}_u{self._uc}", shape, dt))

    def inc(self, instr, sem, k=1, eng=None):
        self.sem(sem)
        if eng is not None:
            instr = eng.drain()
        instr.then_inc(self.S[sem], k)
        self.n[sem] += k
        return self.n[sem]

    def dma(self, eng, sem, out, in_, transpose=False):
        self.sem(sem)
        if transpose:
            ins = eng.dma_start(out=out, in_=in_, transpose=True)
        else:
            ins = eng.dma_start(out=out, in_=in_)
        return self.inc(ins, sem, 16)

    def quiesce(self):
        snap = dict(self.n)
        for en in ("sync", "scalar", "vector", "tensor", "gpsimd"):
            def f(e, snap=snap):
                for s, v in snap.items():
                    if v:
                        e.wait_ge(self.S[s], v)
            getattr(self.blk, en)(f)

    def _qws(self, par, nb, bf):
        if bf:
            return (self.qam[par][:].bitcast(BF16)[:, 0:nb],
                    self.qse[par][:].bitcast(BF16)[:, 0:nb],
                    self.qiv[par][:].bitcast(BF16)[:, 0:nb],
                    self.qsc[par][:].bitcast(BF16)[:, 0:nb])
        return (self.qam[par][:, 0:nb], self.qse[par][:, 0:nb],
                self.qiv[par][:, 0:nb], self.qsc[par][:, 0:nb])

    def emit_quant8(self, v, src, dst, n, par, bf=False):
        nb = n // 32
        am, se, iv, sc = self._qws(par, nb, bf)
        idt, msk = (I16, 0x7F80) if bf else (I32, EXPMASK)
        q8 = self.q8[:, 0:n]
        bv = lambda ap: ap.rearrange("p (nb b) -> p nb b", b=32)
        bb = lambda ap: ap.unsqueeze(2).broadcast_to([128, nb, 32])
        with self.nc.allow_low_precision(reason="pow2 scales exact in bf16"):
            v.tensor_reduce(am, bv(src), AXX, ALU.max, apply_absolute_value=True)
            v.drain()
            v.tensor_scalar(se.bitcast(idt), am.bitcast(idt), msk, None,
                            ALU.bitwise_and)
            v.drain()
            v.reciprocal(iv, se)
            v.tensor_scalar(sc, se, 1.0 / 64.0, None, ALU.mult)
            v.drain()
            v.scalar_tensor_tensor(bv(q8), bb(iv), 64.0, bv(src), ALU.mult,
                                   ALU.mult)
            return v.tensor_tensor(bv(dst), bv(q8), bb(sc), ALU.mult)

    def emit_quant8_pair(self, v, src0, dst0, src1, dst1, n, bf=False):
        nb = n // 32
        idt, msk = (I16, 0x7F80) if bf else (I32, EXPMASK)
        bv = lambda ap: ap.rearrange("p (nb b) -> p nb b", b=32)
        bb = lambda ap: ap.unsqueeze(2).broadcast_to([128, nb, 32])
        ws = []
        for par in range(2):
            ws.append(self._qws(par, nb, bf) + (self.q8[:, n * par:n * (par + 1)],))
        srcs = (src0, src1)
        dsts = (dst0, dst1)
        with self.nc.allow_low_precision(reason="pow2 scales exact in bf16"):
            for par in range(2):
                v.tensor_reduce(ws[par][0], bv(srcs[par]), AXX, ALU.max,
                                apply_absolute_value=True)
            v.drain()
            for par in range(2):
                v.tensor_scalar(ws[par][1].bitcast(idt), ws[par][0].bitcast(idt),
                                msk, None, ALU.bitwise_and)
            v.drain()
            for par in range(2):
                v.reciprocal(ws[par][2], ws[par][1])
                v.tensor_scalar(ws[par][3], ws[par][1], 1.0 / 64.0, None, ALU.mult)
            v.drain()
            for par in range(2):
                v.scalar_tensor_tensor(bv(ws[par][4]), bb(ws[par][2]), 64.0,
                                       bv(srcs[par]), ALU.mult, ALU.mult)
            ins = None
            for par in range(2):
                ins = v.tensor_tensor(bv(dsts[par]), bv(ws[par][4]),
                                      bb(ws[par][3]), ALU.mult)
            return ins

    def emit_ln0(self, src_ap, par, t, tok_src_sem, tok_src):
        nt = {}

        def fv(v, t=t, par=par):
            v.wait_ge(self.S[tok_src_sem], tok_src)
            for c in range(4):
                v.bn_stats(self.st24[par][:, 6 * c:6 * c + 6],
                           src_ap[:, 512 * c:512 * (c + 1)])
            v.drain()
            ins = v.bn_aggr(self.lnsave[:, 2 * t:2 * t + 2], self.st24[par][:])
            nt['v'] = self.inc(ins, 'dv', eng=v)
        self.blk.vector(fv)

        def fs(s, t=t, par=par):
            s.wait_ge(self.S['dv'], nt['v'])
            ins = s.activation(self.rt[par][:], self.lnsave[:, 2 * t + 1:2 * t + 2],
                               AFT.Sqrt, bias=self.epsc[:, 0:1])
            nt['a'] = self.inc(ins, 'ac')
        self.blk.scalar(fs)
        return nt

    def emit_wcasts(self, i=None):
        chunks = list(range(16)) if i is None else [i]

        def fg(g, chunks=chunks):
            for c in chunks:
                if c < 12:
                    self.dma(g, 'wcst', self.watb_s[512 * c:512 * (c + 1), :],
                             self.watd[512 * c:512 * (c + 1), :])
                else:
                    cc = c - 12
                    self.dma(g, 'wcst', self.wob_s[512 * cc:512 * (cc + 1), :],
                             self.wod[512 * cc:512 * (cc + 1), :])
        self.blk.gpsimd(fg)

    def p0(self):
        nt = {}
        with ExitStack() as es:
            b1f = self.sb("b1f", [1, FF], F32, es)
            b2f = self.sb("b2f", [1, D], F32, es)
            b1h = self.sb("b1h", [1, FF], BF16, es)
            b2h = self.sb("b2h", [1, D], BF16, es)
            b1l = self.sb("b1l", [1, FF], BF16, es)
            b2l = self.sb("b2l", [1, D], BF16, es)

            def fsync(sq):
                self.dma(sq, 'cst', self.ones2[:], self.ones2d[:])
                self.dma(sq, 'cst', self.maskpc[:], self.maskpcd[:])
                self.dma(sq, 'cst', self.epsc[:], self.epscd[:])
                self.dma(sq, 'cst', b1f[:], self.b1v[:])
                nt['b'] = self.dma(sq, 'cst', b2f[:], self.b2v[:])
            self.blk.sync(fsync)

            def fv(v):
                v.wait_ge(self.S['cst'], nt['b'])
                v.tensor_copy(b1h[:], b1f[:])
                v.tensor_copy(b2h[:], b2f[:])
                v.drain()
                v.tensor_tensor(b1f[:], b1f[:], b1h[:], ALU.subtract)
                v.tensor_tensor(b2f[:], b2f[:], b2h[:], ALU.subtract)
                v.drain()
                v.tensor_copy(b1l[:], b1f[:])
                ins = v.tensor_copy(b2l[:], b2f[:])
                nt['v'] = self.inc(ins, 'dv', eng=v)
            self.blk.vector(fv)

            def fst(sp):
                sp.wait_ge(self.S['dv'], nt['v'])
                self.dma(sp, 'cst', self.b1hl_s[0:1, :], b1h[:])
                self.dma(sp, 'cst', self.b1hl_s[1:2, :], b1l[:])
                self.dma(sp, 'cst', self.b2hl_s[0:1, :], b2h[:])
                self.dma(sp, 'cst', self.b2hl_s[1:2, :], b2l[:])
            self.blk.scalar(fst)
        self.quiesce()

    def p_ffn_ab(self, first, x_in):
        NT, T = self.NT, self.T
        with ExitStack() as es:
            lnw = self.sb("lnw", [128, D], F32, es)
            lnb = self.sb("lnb", [128, D], F32, es)
            b1hl = self.sb("b1hl", [2, FF], BF16, es)
            self.q1T = self.sb("q1T", [128, ND * T], BF16, es)
            q1Tv = self.q1T[:].rearrange("p (c t) -> p c t", t=T)
            wbuf = [self.sb(f"wbuf{i}", [128, ND * 512], BF16, es) for i in range(2)]
            wbufv = [w[:].rearrange("p (c o) -> p c o", o=512) for w in wbuf]
            xa = [self.sb(f"xa{i}", [128, D], F32, es) for i in range(2)]
            xbt = self.sb("xbt", [128, D], BF16, es)
            h0 = self.sb("h0", [128, D], BF16, es)
            q1b = [self.sb(f"q1b{i}", [128, D], BF16, es) for i in range(2)]
            wn = [self.sb(f"wn{i}", [128, D], F32, es) for i in range(2)]
            wqv = [w[:].bitcast(BF16)[:, 0:D] for w in wn]
            gsb4 = self.sb("gsb4", [128, 2048], BF16, es)
            q2b4 = self.sb("q2b4", [128, 2048], BF16, es)

            def fsync(sq):
                self.dma(sq, 'cst', lnw[:], self.lnw128[:])
                self.dma(sq, 'cst', lnb[:], self.lnb128[:])
                self.dma(sq, 'cst', b1hl[:], self.b1hl_s[:])
            self.blk.sync(fsync)
            tok_cst = self.n['cst']

            guard_xa = [None, None]
            guard_q1b = [None, None]
            m1tok_sl0 = [None] * NT
            q1l_tok = [None] * NT

            def stageA(t):
                par = t % 2

                def fload(sq, t=t, par=par, g=guard_xa[par]):
                    if g is not None:
                        sq.wait_ge(self.S['dv'], g)
                    self.dma(sq, f'xa{par}', xa[par][:],
                             x_in[128 * t:128 * (t + 1), :])
                self.blk.sync(fload)
                tok_x = self.n[f'xa{par}']

                nt0 = self.emit_ln0(xa[par][:], par, t, f'xa{par}', tok_x)
                nt = {}

                def fvB(v, t=t, par=par, nt0=nt0):
                    v.wait_ge(self.S['ac'], nt0['a'])
                    v.reciprocal(self.lnsave[:, 2 * t + 1:2 * t + 2], self.rt[par][:])
                    v.drain()
                    v.tensor_scalar(xbt[:], xa[par][:],
                                    self.lnsave[:, 2 * t:2 * t + 1],
                                    self.lnsave[:, 2 * t + 1:2 * t + 2],
                                    ALU.subtract, ALU.mult)
                    for c in range(4):
                        v.bn_stats(self.st24[par][:, 6 * c:6 * c + 6],
                                   xbt[:, 512 * c:512 * (c + 1)])
                    v.drain()
                    ins = v.bn_aggr(self.mv2[par][:], self.st24[par][:])
                    nt['v'] = self.inc(ins, 'dv', eng=v)
                self.blk.vector(fvB)

                def fsB(s, par=par):
                    s.wait_ge(self.S['dv'], nt['v'])
                    ins = s.activation(self.rt2[par][:], self.mv2[par][:, 1:2],
                                       AFT.Sqrt, bias=self.epsc[:, 0:1])
                    nt['a'] = self.inc(ins, 'ac')
                self.blk.scalar(fsB)

                def fvC(v, t=t, par=par, gq=guard_q1b[par]):
                    v.wait_ge(self.S['ac'], nt['a'])
                    if gq is not None:
                        v.wait_ge(self.S[f'q1s{par}'], gq)
                    v.wait_ge(self.S['cst'], tok_cst)
                    v.reciprocal(self.rt2[par][:], self.rt2[par][:])
                    v.drain()
                    ut = xa[par][:]
                    v.tensor_scalar(ut, xbt[:], self.mv2[par][:, 0:1],
                                    self.rt2[par][:], ALU.subtract, ALU.mult)
                    v.tensor_tensor(ut, ut, lnw[:], ALU.mult)
                    v.tensor_tensor(h0[:], ut, lnb[:], ALU.add)
                    ins = self.emit_quant8(v, h0[:], q1b[par][:], D, par, bf=True)
                    nt['q'] = self.inc(ins, 'dv', eng=v)
                self.blk.vector(fvC)
                guard_xa[par] = nt['q']

                def fst(sp, t=t, par=par):
                    sp.wait_ge(self.S['dv'], nt['q'])
                    self.dma(sp, f'q1s{par}', self.q1_s[128 * t:128 * (t + 1), :],
                             q1b[par][:])
                self.blk.scalar(fst)
                guard_q1b[par] = self.n[f'q1s{par}']
                tok_q1st = self.n[f'q1s{par}']

                def fldT(sq, t=t, par=par, tok_q1st=tok_q1st,
                         g=(m1tok_sl0[t - 2] if t >= 2 else None)):
                    sq.wait_ge(self.S[f'q1s{par}'], tok_q1st)
                    if g is not None:
                        sq.wait_ge(self.S['pe'], g)
                    self.dma(sq, f'q1l{par}',
                             q1Tv[:, :, 128 * t:128 * (t + 1)],
                             self.q1_s[128 * t:128 * (t + 1), :].rearrange(
                                 "t (c p) -> t c p", p=128), transpose=True)
                    q1l_tok[t] = self.n[f'q1l{par}']
                self.blk.sync(fldT)

            w1_ready = [None, None]
            mm_done = [None, None]
            wn_guard = [None, None]
            wcnt = [0]

            def w_addr(kind, idx):
                if kind == 'w1':
                    sl, rt = idx
                    r0 = 512 * sl + 128 * rt
                    return self.w1[r0:r0 + 128, :], self.w1q_s[r0:r0 + 128, :]
                j, qc = idx // 4, idx % 4
                return (self.w2[128 * j:128 * (j + 1), 2048 * qc:2048 * (qc + 1)],
                        self.w2q_s[128 * j:128 * (j + 1), 2048 * qc:2048 * (qc + 1)])

            def w_pair(itA, itB):
                for par in range(2):
                    self.sem(f'w1s{par}')
                    self.sem(f'w2s{par}')
                items = (itA, itB)
                srcs = []
                dsts = []
                ssems = []
                for par, (kind, idx) in enumerate(items):
                    src, dst = w_addr(kind, idx)
                    srcs.append(src)
                    dsts.append(dst)
                    ssems.append(f'{kind}s{par}')

                    def fload(sq, par=par, src=src, g=wn_guard[par],
                              g1=self.n[f'w1s{par}'], g2=self.n[f'w2s{par}']):
                        if g is not None:
                            sq.wait_ge(self.S['dv'], g)
                        if g1:
                            sq.wait_ge(self.S[f'w1s{par}'], g1)
                        if g2:
                            sq.wait_ge(self.S[f'w2s{par}'], g2)
                        self.dma(sq, f'wn{par}', wn[par][:], src)
                    self.blk.sync(fload)
                tok0 = self.n['wn0']
                tok1 = self.n['wn1']
                nt = {}

                def fv(v, tok0=tok0, tok1=tok1):
                    v.wait_ge(self.S['wn0'], tok0)
                    v.wait_ge(self.S['wn1'], tok1)
                    ins = self.emit_quant8_pair(v, wn[0][:], wqv[0],
                                                wn[1][:], wqv[1], D)
                    nt['q'] = self.inc(ins, 'dv', eng=v)
                self.blk.vector(fv)
                wn_guard[0] = nt['q']
                wn_guard[1] = nt['q']

                def fst(sp):
                    sp.wait_ge(self.S['dv'], nt['q'])
                    self.dma(sp, ssems[0], dsts[0], wqv[0])
                    self.dma(sp, ssems[1], dsts[1], wqv[1])
                self.blk.scalar(fst)

            def w1_load(sl):
                bpar = sl % 2
                w0 = self.n.get('w1s0', 0)
                w1c = self.n.get('w1s1', 0)

                def fload(sq, sl=sl, bpar=bpar, mmd=mm_done[bpar], w0=w0, w1c=w1c):
                    if first:
                        if w0:
                            sq.wait_ge(self.S['w1s0'], w0)
                        if w1c:
                            sq.wait_ge(self.S['w1s1'], w1c)
                    if mmd is not None:
                        sq.wait_ge(self.S['pe'], mmd)
                    r0 = 512 * sl
                    self.dma(sq, f'wb{bpar}', wbufv[bpar],
                             self.w1q_s[r0:r0 + 512, :].rearrange(
                                 "o (c p) -> o c p", p=128), transpose=True)
                    w1_ready[bpar] = self.n[f'wb{bpar}']
                self.blk.sync(fload)

            psm1 = [self.psA, self.psB, self.psC, self.psD]
            gelu_done = [None] * 4
            qpair_guard = [None]
            q2st_done = [None, None]
            q2cnt = [0]

            def m1_tile(sl, t):
                par4 = t % 4
                bpar = sl % 2
                pmm = psm1[par4]

                def fmm(te, sl=sl, t=t, bpar=bpar, pmm=pmm, wtok=w1_ready[bpar],
                        g=gelu_done[par4], ql=q1l_tok[t], qpar=t % 2):
                    te.wait_ge(self.S['cst'], tok_cst)
                    te.wait_ge(self.S[f'wb{bpar}'], wtok)
                    te.wait_ge(self.S[f'q1l{qpar}'], ql)
                    if g is not None:
                        te.wait_ge(self.S['ac'], g)
                    for c in range(ND):
                        te.matmul(pmm[:], q1Tv[:, c, 128 * t:128 * (t + 1)],
                                  wbufv[bpar][:, c, :], start=(c == 0), stop=False)
                    ins = te.matmul(pmm[:], self.ones2[:],
                                    b1hl[:, 512 * sl:512 * (sl + 1)],
                                    start=False, stop=True)
                    self.inc(ins, 'pe')
                self.blk.tensor(fmm)
                tok_mm = self.n['pe']
                if t == NT - 1:
                    mm_done[bpar] = tok_mm
                if sl == 0:
                    m1tok_sl0[t] = tok_mm

                if t % 2 == 1:
                    nt = {}

                    def fg(s, t=t, tok_mm=tok_mm,
                           g=(qpair_guard[0] if (first and t % 4 == 1)
                              or not first else None)):
                        s.wait_ge(self.S['pe'], tok_mm)
                        if g is not None:
                            s.wait_ge(self.S['dv'], g)
                        s.activation(gsb4[:, 512 * ((t - 1) % 4):512 * ((t - 1) % 4) + 512],
                                     psm1[(t - 1) % 4][:], AFT.Gelu_apprx_tanh)
                        ins = s.activation(gsb4[:, 512 * (t % 4):512 * (t % 4) + 512],
                                           psm1[t % 4][:], AFT.Gelu_apprx_tanh)
                        nt['g'] = self.inc(ins, 'ac')
                    self.blk.scalar(fg)
                    gelu_done[(t - 1) % 4] = nt['g']
                    gelu_done[t % 4] = nt['g']

                if first and t % 4 == 3:
                    ntq = {}

                    def fq(v, g=nt['g'], g0=q2st_done[0]):
                        v.wait_ge(self.S['ac'], g)
                        if g0 is not None:
                            v.wait_ge(self.S['q2s0'], g0)
                        ins = self.emit_quant8(v, gsb4[:], q2b4[:], 2048, 0,
                                               bf=True)
                        ntq['q'] = self.inc(ins, 'dv', eng=v)
                    self.blk.vector(fq)
                    qpair_guard[0] = ntq['q']

                    def fst(sp, sl=sl, t=t, tok=ntq['q']):
                        sp.wait_ge(self.S['dv'], tok)
                        for k in range(4):
                            self.dma(sp, 'q2s0',
                                     self.q2_s[128 * (t - 3 + k):128 * (t - 2 + k),
                                               512 * sl:512 * (sl + 1)],
                                     q2b4[:, 512 * k:512 * (k + 1)])
                    self.blk.scalar(fst)
                    q2st_done[0] = self.n['q2s0']
                elif (not first) and t % 2 == 1:
                    ntq = {}
                    s0 = 512 * ((t - 1) % 4)
                    s1 = 512 * (t % 4)

                    def fq(v, g=nt['g'], g0=q2st_done[0], s0=s0, s1=s1):
                        v.wait_ge(self.S['ac'], g)
                        if g0 is not None:
                            v.wait_ge(self.S['q2s0'], g0)
                        ins = self.emit_quant8_pair(
                            v, gsb4[:, s0:s0 + 512], q2b4[:, 0:512],
                            gsb4[:, s1:s1 + 512], q2b4[:, 512:1024], 512, bf=True)
                        ntq['q'] = self.inc(ins, 'dv', eng=v)
                    self.blk.vector(fq)
                    qpair_guard[0] = ntq['q']

                    def fst(sp, sl=sl, t=t, tok=ntq['q']):
                        sp.wait_ge(self.S['dv'], tok)
                        self.dma(sp, 'q2s0',
                                 self.q2_s[128 * (t - 1):128 * t,
                                           512 * sl:512 * (sl + 1)],
                                 q2b4[:, 0:512])
                        self.dma(sp, 'q2s0',
                                 self.q2_s[128 * t:128 * (t + 1),
                                           512 * sl:512 * (sl + 1)],
                                 q2b4[:, 512:1024])
                    self.blk.scalar(fst)
                    q2st_done[0] = self.n['q2s0']

            witems = {sl: [] for sl in range(NSL)}
            if first:
                for sl in range(NSL - 1):
                    witems[sl].append((('w1', (sl + 1, 0)), ('w1', (sl + 1, 1))))
                    witems[sl].append((('w1', (sl + 1, 2)), ('w1', (sl + 1, 3))))
                for h in range(0, 32, 2):
                    witems[min(1 + h // 2, NSL - 1)].append((('w2', h), ('w2', h + 1)))

            def emit_wpair(pr):
                w_pair(pr[0], pr[1])
                if pr[1][0] == 'w1' and pr[1][1][1] == 3:
                    w1_load(pr[1][1][0])

            wl0 = witems[0]
            done0 = 0
            for t in range(NT):
                if 'pa' in SERIAL:
                    self.quiesce()
                stageA(t)
                if t == 0:
                    if first:
                        w_pair(('w1', (0, 0)), ('w1', (0, 1)))
                        w_pair(('w1', (0, 2)), ('w1', (0, 3)))
                    else:
                        w1_load(0)
                        w1_load(1)
                elif t == 1 and first:
                    w1_load(0)
                else:
                    want = (t * len(wl0) + NT - 2) // (NT - 1)
                    while done0 < min(want, len(wl0)):
                        emit_wpair(wl0[done0])
                        done0 += 1
                if t >= 1:
                    m1_tile(0, t - 1)
                if not first and t >= 2:
                    m1_tile(1, t - 2)
            m1_tile(0, NT - 1)
            while done0 < len(wl0):
                emit_wpair(wl0[done0])
                done0 += 1
            if not first:
                m1_tile(1, NT - 2)
                m1_tile(1, NT - 1)
                w1_load(2)

            for sl in range(2 if not first else 1, NSL):
                wl = witems[sl]
                done = 0
                if first and sl >= 1:
                    self.emit_wcasts(sl - 1)
                for t in range(NT):
                    if 'pb' in SERIAL:
                        self.quiesce()
                    if not first and t == 0 and sl + 1 < NSL:
                        w1_load(sl + 1)
                    want = ((t + 1) * len(wl) + NT - 1) // NT
                    while done < min(want, len(wl)):
                        emit_wpair(wl[done])
                        done += 1
                    m1_tile(sl, t)
                while done < len(wl):
                    emit_wpair(wl[done])
                    done += 1
            if first:
                self.emit_wcasts(15)
        self.quiesce()

    def p_ffn_c(self, first, x_in, x_out, out_sem):
        NT, T = self.NT, self.T
        with ExitStack() as es:
            ring = self.sb("ring", [128, NF * 1024], BF16, es)
            wnC = [self.sb(f"wnC{i}", [128, D], F32, es) for i in range(2)] \
                if first else None
            wqvC = [w[:].bitcast(BF16)[:, 0:D] for w in wnC] if first else None
            wn_guardC = [None, None]
            ringv = ring[:].rearrange("p (c d) -> p c d", d=1024)
            b2hl = self.sb("b2hl", [2, D], BF16, es)
            q2t = [self.sb(f"q2t{i}", [128, NF * 128], BF16, es) for i in range(2)]
            q2tv = [r[:].rearrange("p (c t) -> p c t", t=128) for r in q2t]
            xpc = [self.sb(f"xpc{i}", [128, 1024], F32, es) for i in range(2)]
            xbp1 = self.sb("xbp", [128, 1024], BF16, es)
            xbp = [xbp1, xbp1]
            h2p = self.sb("h2p", [128, 1024], BF16, es)
            xo = [self.sb(f"xo{i}", [128, 1024], F32, es) for i in range(2)]

            def fsync0(sq):
                self.dma(sq, 'cst', b2hl[:], self.b2hl_s[:])
            self.blk.sync(fsync0)
            tok_cst = self.n['cst']

            q2_tok = [None, None]
            q2_guard = [None, None]
            ps_guard = [None, None]
            xpc_guard = [None, None]
            xo_guard = [None, None]
            ring_guard = [None]
            q2s_tot = (self.n.get('q2s0', 0), self.n.get('q2s1', 0))
            w2s_tot = (self.n.get('w2s0', 0), self.n.get('w2s1', 0))

            def ring_load(dh):
                toks = {}
                wt = (self.n.get('w2s0', 0), self.n.get('w2s1', 0))

                def f(sq, dh=dh, g=ring_guard[0], wt=wt):
                    if first:
                        if wt[0]:
                            sq.wait_ge(self.S['w2s0'], wt[0])
                        if wt[1]:
                            sq.wait_ge(self.S['w2s1'], wt[1])
                    if g is not None:
                        sq.wait_ge(self.S['pe'], g)
                    src = self.w2q_s[1024 * dh:1024 * (dh + 1), :].rearrange(
                        "d (c p) -> d c p", p=128)
                    toks['a'] = self.dma(sq, 'rg', ringv[:, 0:32, :],
                                         src[:, 0:32, :], transpose=True)
                    toks['b'] = self.dma(sq, 'rg2', ringv[:, 32:64, :],
                                         src[:, 32:64, :], transpose=True)
                self.blk.sync(f)
                return (toks['a'], toks['b'])

            def w_pairC(hA, hB):
                for par in range(2):
                    self.sem(f'w2s{par}')
                hs = (hA, hB)

                def fload(sq, hs=hs, g0=wn_guardC[0], g1=wn_guardC[1],
                          s0=self.n['w2s0'], s1=self.n['w2s1']):
                    for par in range(2):
                        g = (g0, g1)[par]
                        if g is not None:
                            sq.wait_ge(self.S['dv'], g)
                        if (s0, s1)[par]:
                            sq.wait_ge(self.S[f'w2s{par}'], (s0, s1)[par])
                        j, qc = hs[par] // 4, hs[par] % 4
                        self.dma(sq, f'wnc{par}', wnC[par][:],
                                 self.w2[128 * j:128 * (j + 1),
                                         2048 * qc:2048 * (qc + 1)])
                self.blk.sync(fload)
                tok0 = self.n['wnc0']
                tok1 = self.n['wnc1']
                nt = {}

                def fv(v, tok0=tok0, tok1=tok1):
                    v.wait_ge(self.S['wnc0'], tok0)
                    v.wait_ge(self.S['wnc1'], tok1)
                    ins = self.emit_quant8_pair(v, wnC[0][:], wqvC[0],
                                                wnC[1][:], wqvC[1], D)
                    nt['q'] = self.inc(ins, 'dv', eng=v)
                self.blk.vector(fv)
                wn_guardC[0] = nt['q']
                wn_guardC[1] = nt['q']

                def fst(sp, hs=hs):
                    sp.wait_ge(self.S['dv'], nt['q'])
                    for par in range(2):
                        j, qc = hs[par] // 4, hs[par] % 4
                        self.dma(sp, f'w2s{par}',
                                 self.w2q_s[128 * j:128 * (j + 1),
                                            2048 * qc:2048 * (qc + 1)],
                                 wqvC[par][:])
                self.blk.scalar(fst)

            def q2t_load(t):
                par = t % 2

                def f(sq, t=t, par=par, g=q2_guard[par]):
                    sq.wait_ge(self.S['q2s0'], q2s_tot[0])
                    if q2s_tot[1]:
                        sq.wait_ge(self.S['q2s1'], q2s_tot[1])
                    if g is not None:
                        sq.wait_ge(self.S['pe'], g)
                    self.dma(sq, f'q2l{par}', q2tv[par],
                             self.q2_s[128 * t:128 * (t + 1), :].rearrange(
                                 "t (c p) -> t c p", p=128), transpose=True)
                    q2_tok[par] = self.n[f'q2l{par}']
                self.blk.sync(f)

            for dh in range(2):
                tok_rg = ring_load(dh)
                if dh == 0:
                    q2t_load(0)
                for t in range(NT):
                    if 'pc' in SERIAL:
                        self.quiesce()
                    par = t % 2
                    pa = (self.psA, self.psS1)[par]
                    pb = (self.psB, self.psS2)[par]
                    if t + 1 < NT:
                        q2t_load(t + 1)
                    elif dh == 0:
                        q2t_load(0)
                    if first and dh == 0:
                        w_pairC(32 + 2 * t, 33 + 2 * t)

                    def fmm1(te, t=t, par=par, pa=pa, pb=pb, tok_rg=tok_rg,
                             q2r=q2_tok[par], g=ps_guard[par]):
                        te.wait_ge(self.S['rg'], tok_rg[0])
                        te.wait_ge(self.S[f'q2l{par}'], q2r)
                        te.wait_ge(self.S['cst'], tok_cst)
                        if g is not None:
                            te.wait_ge(self.S['dv'], g)
                        for c in range(32):
                            te.matmul(pa[:], q2tv[par][:, c, :],
                                      ringv[:, c, 0:512], start=(c == 0), stop=False)
                            te.matmul(pb[:], q2tv[par][:, c, :],
                                      ringv[:, c, 512:1024], start=(c == 0),
                                      stop=False)
                    self.blk.tensor(fmm1)

                    def fmm(te, dh=dh, par=par, pa=pa, pb=pb, tok_rg=tok_rg):
                        te.wait_ge(self.S['rg2'], tok_rg[1])
                        for c in range(32, NF):
                            te.matmul(pa[:], q2tv[par][:, c, :],
                                      ringv[:, c, 0:512], start=False, stop=False)
                            te.matmul(pb[:], q2tv[par][:, c, :],
                                      ringv[:, c, 512:1024], start=False,
                                      stop=False)
                        o0 = 1024 * dh
                        te.matmul(pa[:], self.ones2[:],
                                  b2hl[:, o0:o0 + 512], start=False, stop=True)
                        ins = te.matmul(pb[:], self.ones2[:],
                                        b2hl[:, o0 + 512:o0 + 1024],
                                        start=False, stop=True)
                        self.inc(ins, 'pe')
                    self.blk.tensor(fmm)
                    tok_mm = self.n['pe']
                    q2_guard[par] = tok_mm
                    if t == NT - 1:
                        ring_guard[0] = tok_mm

                    def fsx(sq, dh=dh, t=t, par=par, g=xpc_guard[par]):
                        if g is not None:
                            sq.wait_ge(self.S['dv'], g)
                        self.dma(sq, f'xp{par}', xpc[par][:],
                                 x_in[128 * t:128 * (t + 1),
                                      1024 * dh:1024 * (dh + 1)])
                    self.blk.sync(fsx)
                    tok_xp = self.n[f'xp{par}']
                    nt = {}

                    def fv(v, dh=dh, t=t, par=par, pa=pa, pb=pb, tok_mm=tok_mm,
                           tok_xp=tok_xp, gxo=xo_guard[par]):
                        v.wait_ge(self.S['pe'], tok_mm)
                        v.wait_ge(self.S[f'xp{par}'], tok_xp)
                        if gxo is not None:
                            v.wait_ge(self.S[f'{out_sem}{par}'], gxo)
                        v.tensor_scalar(xbp[par][:], xpc[par][:],
                                        self.lnsave[:, 2 * t:2 * t + 1],
                                        self.lnsave[:, 2 * t + 1:2 * t + 2],
                                        ALU.subtract, ALU.mult)
                        v.tensor_copy(h2p[:, 0:512], pa[:])
                        v.tensor_copy(h2p[:, 512:1024], pb[:])
                        v.tensor_tensor(h2p[:], xbp[par][:], h2p[:], ALU.add)
                        ins = v.scalar_tensor_tensor(xo[par][:], h2p[:], 0.5,
                                                     xpc[par][:], ALU.mult, ALU.add)
                        nt['u'] = self.inc(ins, 'dv', eng=v)
                    self.blk.vector(fv)
                    ps_guard[par] = nt['u']
                    xpc_guard[par] = nt['u']

                    def fso(sp, dh=dh, t=t, par=par, tok=nt['u']):
                        sp.wait_ge(self.S['dv'], tok)
                        self.dma(sp, f'{out_sem}{par}',
                                 x_out[128 * t:128 * (t + 1),
                                       1024 * dh:1024 * (dh + 1)],
                                 xo[par][:])
                        xo_guard[par] = self.n[f'{out_sem}{par}']
                    self.blk.scalar(fso)
        self.quiesce()

    def p_attn_qkv(self, x_in):
        NT, T = self.NT, self.T
        with ExitStack() as es:
            self.aT = self.sb("aT", [128, ND * T], BF16, es)
            aTv = self.aT[:].rearrange("p (c t) -> p c t", t=T)
            xa = [self.sb(f"xa{i}", [128, D], F32, es) for i in range(2)]
            at = [self.sb(f"at{i}", [128, D], BF16, es) for i in range(2)]
            wsl = [self.sb(f"wsl{i}", [128, ND * 512], BF16, es) for i in range(2)]
            wslv = [w[:].rearrange("p (c o) -> p c o", o=512) for w in wsl]
            qkvb = [self.sb(f"qkvb{i}", [128, 512], BF16, es) for i in range(2)]

            guard_xa = [None, None]
            guard_at = [None, None]
            aTl_tok = [None] * NT
            e0tok = [None] * NT

            def stageD(t):
                par = t % 2

                def fload(sq, t=t, par=par, g=guard_xa[par]):
                    if g is not None:
                        sq.wait_ge(self.S['dv'], g)
                    self.dma(sq, f'xa{par}', xa[par][:],
                             x_in[128 * t:128 * (t + 1), :])
                self.blk.sync(fload)
                tok_x = self.n[f'xa{par}']
                nt0 = self.emit_ln0(xa[par][:], par, t, f'xa{par}', tok_x)
                nt = {}

                def fvB(v, t=t, par=par, nt0=nt0, gq=guard_at[par]):
                    v.wait_ge(self.S['ac'], nt0['a'])
                    if gq is not None:
                        v.wait_ge(self.S[f'ats{par}'], gq)
                    v.reciprocal(self.lnsave[:, 2 * t + 1:2 * t + 2], self.rt[par][:])
                    v.drain()
                    ins = v.tensor_scalar(at[par][:], xa[par][:],
                                          self.lnsave[:, 2 * t:2 * t + 1],
                                          self.lnsave[:, 2 * t + 1:2 * t + 2],
                                          ALU.subtract, ALU.mult)
                    nt['v'] = self.inc(ins, 'dv', eng=v)
                self.blk.vector(fvB)
                guard_xa[par] = nt['v']

                def fst(sp, t=t, par=par):
                    sp.wait_ge(self.S['dv'], nt['v'])
                    self.dma(sp, f'ats{par}', self.at_s[128 * t:128 * (t + 1), :],
                             at[par][:])
                self.blk.scalar(fst)
                guard_at[par] = self.n[f'ats{par}']
                tok_st = self.n[f'ats{par}']

                def fldT(sq, t=t, par=par, tok_st=tok_st,
                         g=(e0tok[t - 2] if t >= 2 else None)):
                    sq.wait_ge(self.S[f'ats{par}'], tok_st)
                    if g is not None:
                        sq.wait_ge(self.S['pe'], g)
                    self.dma(sq, f'atl{par}',
                             aTv[:, :, 128 * t:128 * (t + 1)],
                             self.at_s[128 * t:128 * (t + 1), :].rearrange(
                                 "t (c p) -> t c p", p=128), transpose=True)
                    aTl_tok[t] = self.n[f'atl{par}']
                self.blk.sync(fldT)

            w_ready = [None, None]
            mm_done = [None, None]
            store_done = [None, None]
            wcst_tok = self.n['wcst']

            def wsl_load(os):
                par = os % 2

                def f(sq, os=os, par=par, g=mm_done[par]):
                    sq.wait_ge(self.S['wcst'], wcst_tok)
                    if g is not None:
                        sq.wait_ge(self.S['pe'], g)
                    self.dma(sq, f'ws{par}', wslv[par],
                             self.watb_s[512 * os:512 * (os + 1), :].rearrange(
                                 "o (c p) -> o c p", p=128), transpose=True)
                    w_ready[par] = self.n[f'ws{par}']
                self.blk.sync(f)

            psq = [self.psA, self.psB, self.psC, self.psD]
            ps_guard = [None] * 4

            def e_tile(os, t):
                par = os % 2
                par4 = t % 4
                pmm = psq[par4]

                def fmm(te, os=os, t=t, par=par, pmm=pmm, wt=w_ready[par],
                        g=ps_guard[par4], al=aTl_tok[t], apar=t % 2):
                    te.wait_ge(self.S[f'ws{par}'], wt)
                    te.wait_ge(self.S[f'atl{apar}'], al)
                    if g is not None:
                        te.wait_ge(self.S['dv'], g)
                    for c in range(ND):
                        ins = te.matmul(pmm[:], aTv[:, c, 128 * t:128 * (t + 1)],
                                        wslv[par][:, c, :],
                                        start=(c == 0), stop=(c == ND - 1))
                    self.inc(ins, 'pe')
                self.blk.tensor(fmm)
                tok_mm = self.n['pe']
                if t == self.NT - 1:
                    mm_done[par] = tok_mm
                if os == 0:
                    e0tok[t] = tok_mm
                epar = t % 2
                nt = {}

                def fv(v, os=os, pmm=pmm, epar=epar, tok_mm=tok_mm,
                       g=store_done[epar]):
                    v.wait_ge(self.S['pe'], tok_mm)
                    if g is not None:
                        v.wait_ge(self.S[f'qks{epar}'], g)
                    if os < 4:
                        ins = v.tensor_scalar(qkvb[epar][:], pmm[:], 1.0 / 128.0,
                                              None, ALU.mult)
                    else:
                        ins = v.tensor_copy(qkvb[epar][:], pmm[:])
                    nt['e'] = self.inc(ins, 'dv', eng=v)
                self.blk.vector(fv)
                ps_guard[par4] = nt['e']

                def fst(sp, os=os, t=t, epar=epar, tok=nt['e']):
                    sp.wait_ge(self.S['dv'], tok)
                    self.dma(sp, f'qks{epar}',
                             self.qkv_s[128 * t:128 * (t + 1),
                                        512 * os:512 * (os + 1)],
                             qkvb[epar][:])
                self.blk.scalar(fst)
                store_done[epar] = self.n[f'qks{epar}']

            for t in range(NT):
                if 'pd' in SERIAL:
                    self.quiesce()
                stageD(t)
                if t == 0:
                    wsl_load(0)
                if t == 1:
                    wsl_load(1)
                if t >= 1:
                    e_tile(0, t - 1)
                if t >= 2:
                    e_tile(1, t - 2)
            e_tile(0, NT - 1)
            e_tile(1, NT - 2)
            e_tile(1, NT - 1)
            wsl_load(2)
            for os in range(2, 12):
                for t in range(NT):
                    if 'pq' in SERIAL:
                        self.quiesce()
                    if t == 0 and os + 1 < 12:
                        wsl_load(os + 1)
                    e_tile(os, t)
        self.quiesce()

    def p_attn_win(self, x_in, x_out, out_sem):
        NT, T = self.NT, self.T
        with ExitStack() as es:
            oT = self.sb("oT", [128, ND * T], BF16, es)
            oTv = oT[:].rearrange("p (c t) -> p c t", t=T)
            kw = [self.sb(f"kw{i}", [128, ND * 128], BF16, es) for i in range(3)]
            qw = [self.sb(f"qw{i}", [128, ND * 128], BF16, es) for i in range(3)]
            vw = [self.sb(f"vw{i}", [128, D], BF16, es) for i in range(3)]
            kwv = [k[:].rearrange("p (c j) -> p c j", j=128) for k in kw]
            qwv = [k[:].rearrange("p (c j) -> p c j", j=128) for k in qw]
            rpc = self.sb("rpc", [128, 256], F32, es)
            at2 = self.sb("at2", [128, 256], BF16, es)

            kv_tok = [None, None, None]
            avtoks = [None] * (self.NT + 1)
            last_av = [None]
            relu_guard = [None]
            mask_guard = [None]
            evac_guard = [None, None]
            qks_tot = (self.n.get('qks0', 0), self.n.get('qks1', 0))

            def fz(v):
                v.memset(rpc[:], 0.0)
                ins = v.drain()
                self.inc(ins, 'dv')
            self.blk.vector(fz)

            def loadw(w):
                par = w % 3
                g = avtoks[w - 2] if w >= 2 else None

                def f(sq, w=w, par=par, g=g):
                    sq.wait_ge(self.S['qks0'], qks_tot[0])
                    sq.wait_ge(self.S['qks1'], qks_tot[1])
                    if g is not None:
                        sq.wait_ge(self.S['pe'], g)
                    row = self.qkv_s[128 * w:128 * (w + 1), :]
                    self.dma(sq, f'kv{par}', qwv[par],
                             row[:, 0:D].rearrange("t (c p) -> t c p", p=128),
                             transpose=True)
                    self.dma(sq, f'kv{par}', kwv[par],
                             row[:, D:2 * D].rearrange("t (c p) -> t c p", p=128),
                             transpose=True)
                    self.dma(sq, f'kv{par}', vw[par][:], row[:, 2 * D:3 * D])
                    kv_tok[par] = self.n[f'kv{par}']
                self.blk.sync(f)

            loadw(0)
            loadw(1)
            relu_guard2 = [None, None]
            for w in range(NT):
                if 'pw' in SERIAL:
                    self.quiesce()
                par = w % 3
                prv = (w - 1) % 3
                wp = w % 2
                pS1 = (self.psS1, self.psC)[wp]
                pS2 = (self.psS2, self.psD)[wp]
                havep = w > 0

                def fsim(te, w=w, par=par, prv=prv, havep=havep, pS1=pS1, pS2=pS2,
                         rg=relu_guard2[wp], kt=kv_tok[par]):
                    te.wait_ge(self.S[f'kv{par}'], kt)
                    if rg is not None:
                        te.wait_ge(self.S['ac'], rg)
                    if havep:
                        for c in range(ND):
                            te.matmul(pS1[:, 0:128], kwv[prv][:, c, :],
                                      qwv[par][:, c, :], start=(c == 0),
                                      stop=(c == ND - 1))
                    for c in range(ND):
                        ins = te.matmul(pS2[:, 0:128], kwv[par][:, c, :],
                                        qwv[par][:, c, :], start=(c == 0),
                                        stop=(c == ND - 1))
                    self.inc(ins, 'pe')
                self.blk.tensor(fsim)
                tok_sim = self.n['pe']

                def frelu(s, havep=havep, pS1=pS1, pS2=pS2, tok_sim=tok_sim,
                          mg=mask_guard[0]):
                    s.wait_ge(self.S['pe'], tok_sim)
                    if mg is not None:
                        s.wait_ge(self.S['dv'], mg)
                    if havep:
                        s.activation(rpc[:, 0:128], pS1[:, 0:128], AFT.Relu)
                    ins = s.activation(rpc[:, 128:256], pS2[:, 0:128],
                                       AFT.Relu)
                    self.inc(ins, 'ac')
                self.blk.scalar(frelu)
                relu_guard2[wp] = self.n['ac']
                tok_relu = self.n['ac']
                nt = {}

                def fmask(v, tok_relu=tok_relu, avg=last_av[0]):
                    v.wait_ge(self.S['ac'], tok_relu)
                    if avg is not None:
                        v.wait_ge(self.S['pe'], avg)
                    v.tensor_tensor(rpc[:], rpc[:], self.maskpc[:], ALU.mult)
                    v.drain()
                    ins = v.tensor_tensor(at2[:], rpc[:], rpc[:], ALU.mult)
                    nt['m'] = self.inc(ins, 'dv', eng=v)
                self.blk.vector(fmask)
                mask_guard[0] = nt['m']
                tok_at = nt['m']

                for grp in range(4):
                    psv = (self.psV1, self.psV2)[grp % 2]
                    gpar = grp % 2

                    def fav(te, par=par, prv=prv, havep=havep, grp=grp, psv=psv,
                            tok_at=tok_at, ev=evac_guard[grp % 2]):
                        te.wait_ge(self.S['dv'], tok_at)
                        if ev is not None:
                            te.wait_ge(self.S['dv'], ev)
                        for j in range(4):
                            c = 4 * grp + j
                            if havep:
                                te.matmul(psv[:, 128 * j:128 * (j + 1)],
                                          vw[prv][:, 128 * c:128 * (c + 1)],
                                          at2[:, 0:128], start=True, stop=False)
                            ins = te.matmul(psv[:, 128 * j:128 * (j + 1)],
                                            vw[par][:, 128 * c:128 * (c + 1)],
                                            at2[:, 128:256], start=(not havep),
                                            stop=True)
                        self.inc(ins, 'pe')
                    self.blk.tensor(fav)
                    tok_av = self.n['pe']
                    if grp == 3:
                        last_av[0] = tok_av
                        avtoks[w] = tok_av

                    def fve(v, w=w, grp=grp, psv=psv, tok_av=tok_av):
                        v.wait_ge(self.S['pe'], tok_av)
                        ins = v.tensor_copy(
                            oTv[:, 4 * grp:4 * grp + 4, 128 * w:128 * (w + 1)],
                            psv[:].rearrange("p (j tt) -> p j tt", tt=128))
                        self.inc(ins, 'dv', eng=v)
                    self.blk.vector(fve)
                    evac_guard[gpar] = self.n['dv']
                if w + 2 < NT:
                    loadw(w + 2)
            tok_oT = self.n['dv']

            wob = [self.sb(f"wob{i}", [128, ND * 512], BF16, es) for i in range(2)]
            wobv = [w[:].rearrange("p (c o) -> p c o", o=512) for w in wob]
            xpc = [self.sb(f"xpg{i}", [128, 512], F32, es) for i in range(2)]
            xo = [self.sb(f"xog{i}", [128, 512], F32, es) for i in range(2)]
            wo_ready = [None, None]
            wo_guard = [None, None]
            xo_guard = [None, None]
            xpc_guard = [None, None]
            psg = [None, None]
            wcst_tok = self.n['wcst']

            def wo_load(os):
                par = os % 2

                def f(sq, os=os, par=par, g=wo_guard[par]):
                    sq.wait_ge(self.S['wcst'], wcst_tok)
                    if g is not None:
                        sq.wait_ge(self.S['pe'], g)
                    self.dma(sq, f'wo{par}', wobv[par],
                             self.wob_s[512 * os:512 * (os + 1), :].rearrange(
                                 "o (c p) -> o c p", p=128), transpose=True)
                    wo_ready[par] = self.n[f'wo{par}']
                self.blk.sync(f)

            wo_load(0)
            for os in range(4):
                par = os % 2
                if os + 1 < 4:
                    wo_load(os + 1)
                for t in range(NT):
                    if 'pg' in SERIAL:
                        self.quiesce()
                    pmm = (self.psA, self.psB)[t % 2]
                    ppar = t % 2

                    def fsx(sq, os=os, t=t, ppar=ppar, g=xpc_guard[ppar]):
                        if g is not None:
                            sq.wait_ge(self.S['dv'], g)
                        self.dma(sq, f'xp{ppar}', xpc[ppar][:],
                                 x_in[128 * t:128 * (t + 1),
                                      512 * os:512 * (os + 1)])
                    self.blk.sync(fsx)
                    tok_xp = self.n[f'xp{ppar}']

                    def fmm(te, os=os, t=t, pmm=pmm, par=par, wt=wo_ready[par],
                            g=psg[ppar]):
                        te.wait_ge(self.S[f'wo{par}'], wt)
                        if os == 0 and t == 0:
                            te.wait_ge(self.S['dv'], tok_oT)
                        if g is not None:
                            te.wait_ge(self.S['dv'], g)
                        for c in range(ND):
                            ins = te.matmul(pmm[:], oTv[:, c, 128 * t:128 * (t + 1)],
                                            wobv[par][:, c, :],
                                            start=(c == 0), stop=(c == ND - 1))
                        self.inc(ins, 'pe')
                    self.blk.tensor(fmm)
                    tok_mm = self.n['pe']
                    if t == NT - 1:
                        wo_guard[par] = tok_mm
                    nt = {}

                    def fvu(v, pmm=pmm, ppar=ppar, tok_mm=tok_mm, tok_xp=tok_xp,
                            g=xo_guard[ppar]):
                        v.wait_ge(self.S['pe'], tok_mm)
                        v.wait_ge(self.S[f'xp{ppar}'], tok_xp)
                        if g is not None:
                            v.wait_ge(self.S[f'{out_sem}{ppar}'], g)
                        ins = v.tensor_tensor(xo[ppar][:], pmm[:], xpc[ppar][:],
                                              ALU.add)
                        nt['u'] = self.inc(ins, 'dv', eng=v)
                    self.blk.vector(fvu)
                    psg[ppar] = nt['u']
                    xpc_guard[ppar] = nt['u']

                    def fso(sp, os=os, t=t, ppar=ppar, tok=nt['u']):
                        sp.wait_ge(self.S['dv'], tok)
                        self.dma(sp, f'{out_sem}{ppar}',
                                 x_out[128 * t:128 * (t + 1),
                                       512 * os:512 * (os + 1)],
                                 xo[ppar][:])
                        xo_guard[ppar] = self.n[f'{out_sem}{ppar}']
                    self.blk.scalar(fso)
        self.quiesce()

    def build(self):
        self.p0()
        self.p_ffn_ab(True, self.x0)
        self.p_ffn_c(True, self.x0, self.x1s, 'p')
        self.p_attn_qkv(self.x1s)
        self.p_attn_win(self.x1s, self.x2s, 'o')
        self.p_ffn_ab(False, self.x2s)
        self.p_ffn_c(False, self.x2s, self.outd, 'f')
        snap = dict(self.n)

        def fin(sq, snap=snap):
            for s, v in snap.items():
                if v:
                    sq.wait_ge(self.S[s], v)
        self.blk.sync(fin)

    def finish(self):
        self.es.close()
        return self.nc


def build_program(T=2048, debug=False):
    mk = MK(T, debug=debug)
    mk.build()
    return mk.finish()


def host_prep(x_full, w_attn, w_o, ln_w, ln_b, W1, b1, W2, b2):
    f = np.float32
    bf = ml_dtypes.bfloat16
    common = {
        "wat": np.ascontiguousarray(w_attn.astype(f, copy=False)),
        "wo": np.ascontiguousarray(w_o.astype(f, copy=False)),
        "w1": np.ascontiguousarray(W1.astype(f, copy=False)),
        "w2": np.ascontiguousarray(W2.astype(f, copy=False)),
        "lnw128": np.ascontiguousarray(np.broadcast_to(ln_w[None, :], (128, D)).astype(f)),
        "lnb128": np.ascontiguousarray(np.broadcast_to(ln_b[None, :], (128, D)).astype(f)),
        "b1v": np.ascontiguousarray(b1[None, :].astype(f)),
        "b2v": np.ascontiguousarray(b2[None, :].astype(f)),
        "ones2d": np.ones((2, 128), dtype=bf),
        "maskpcd": np.ascontiguousarray(np.hstack([
            np.tril(np.ones((128, 128), dtype=f)),
            np.triu(np.ones((128, 128), dtype=f))])),
        "epscd": np.full((128, 1), EPS, dtype=f),
    }
    xs = [np.ascontiguousarray(x_full[i].astype(f, copy=False))
          for i in range(x_full.shape[0])]
    return common, xs


def kernel(L0_attn_x, L0_attn_w_attn, L0_attn_w_o, L0_ffn_ln_w, L0_ffn_ln_b,
           L0_ffn_W1, L0_ffn_b1, L0_ffn_W2, L0_ffn_b2):
    x = np.asarray(L0_attn_x)
    B, T, _ = x.shape
    common, xs = host_prep(x, np.asarray(L0_attn_w_attn), np.asarray(L0_attn_w_o),
                           np.asarray(L0_ffn_ln_w), np.asarray(L0_ffn_ln_b),
                           np.asarray(L0_ffn_W1), np.asarray(L0_ffn_b1),
                           np.asarray(L0_ffn_W2), np.asarray(L0_ffn_b2))
    nc = build_program(T)
    in_maps = [dict(common, x0=xs[i]) for i in range(B)]
    res = run_bass_kernel_spmd(nc, in_maps, core_ids=list(range(B)))
    return np.stack([np.asarray(r["out"]) for r in res.results], axis=0)
